# version 04
# baseline (speedup 1.0000x reference)
import contextlib
import numpy as np

import concourse.bass as bass
import concourse.mybir as mybir
from concourse.bass_utils import run_bass_kernel_spmd
from concourse.aot_env import allow_ici_all_to_all

F32 = mybir.dt.float32
AX = mybir.AxisListType.X
ADD = mybir.AluOpType.add
SUB = mybir.AluOpType.subtract
MUL = mybir.AluOpType.mult
AF = mybir.ActivationFunctionType

SEQ, DM, NH, HD, NCORE = 8192, 2048, 16, 128, 8
HPC = NH // NCORE
TOK = SEQ // NCORE
INTER = 8192
QS = 512
NSTRIP = SEQ // QS
FT = 256
NSUB = TOK // FT
NCH = DM // 128
NIC = INTER // 128
EPS = 1e-5
EBIAS = -20.0
QSC = float(1.0 / np.sqrt(128.0))
RD = float(1.0 / DM)

OWN = {"tensor": "te", "vector": "ve", "scalar": "sc", "gpsimd": "gc"}
NSY = 8
NGP = 4
CONST = ("const",)


class Ctx:

    def __init__(self, eng, E, sems):
        self.eng = eng
        self.E = E
        self.sems = sems
        self.cnt = {k: 0 for k in sems}
        self._lastw = {}
        self._rr = {"sy": 0, "gp": 0}
        self.last_w = {}
        self.readers = {}

    def w(self, sem, val):
        if val is None or val <= 0:
            return
        if self._lastw.get(sem, -1) >= val:
            return
        self._lastw[sem] = val
        if self.E is not None:
            self.E.wait_ge(self.sems[sem], val)

    def _dep(self, eng, reads, writes):
        emitting = (self.eng == eng)
        for k in reads:
            t = self.last_w.get(k)
            if t is not None and not (eng == "tensor" and t[2] == "tensor"):
                if emitting:
                    self.w(t[0], t[1])
        for k in writes:
            t = self.last_w.get(k)
            if t is not None and not (eng == "tensor" and t[2] == "tensor"):
                if emitting:
                    self.w(t[0], t[1])
            for rt in self.readers.get(k, ()):
                if eng == "tensor" and rt[2] == "tensor":
                    continue
                if emitting:
                    self.w(rt[0], rt[1])

    def _fin(self, tok, reads, writes):
        for k in writes:
            self.last_w[k] = tok
            self.readers[k] = []
        for k in reads:
            self.readers.setdefault(k, []).append(tok)

    def op(self, eng, fn, r=(), w=()):
        self._dep(eng, r, w)
        sem = OWN[eng]
        if self.eng == eng:
            fn(self.E).then_inc(self.sems[sem], 1)
        self.cnt[sem] += 1
        tok = (sem, self.cnt[sem], eng)
        self._fin(tok, r, w)
        return tok

    def grp(self, fns, r=(), w=()):
        self._dep("tensor", r, w)
        n = len(fns)
        if self.eng == "tensor":
            for i, fn in enumerate(fns):
                ins = fn(self.E)
                if i == n - 1:
                    ins.then_inc(self.sems["te"], 1)
        self.cnt["te"] += 1
        tok = ("te", self.cnt["te"], "tensor")
        self._fin(tok, r, w)
        return tok

    def dma(self, eng, out, in_, q, r=(), w=()):
        self._dep(eng, r, w)
        n = NSY if q == "sy" else NGP
        lane = self._rr[q]
        self._rr[q] = (lane + 1) % n
        sem = (q, lane)
        if self.eng == eng:
            self.w(sem, self.cnt[sem])
            self.E.dma_start(out=out, in_=in_).then_inc(self.sems[sem], 16)
        self.cnt[sem] += 16
        tok = (sem, self.cnt[sem], eng)
        self._fin(tok, r, w)
        return tok

    def wq(self, q):
        n = NSY if q == "sy" else NGP
        for lane in range(n):
            self.w((q, lane), self.cnt[(q, lane)])

    def barrier(self):
        for sem in ("te", "ve", "sc", "gc", "cc"):
            self.w(sem, self.cnt[sem])
        self.wq("sy")
        self.wq("gp")


def hadamard(n):
    h = np.array([[1.0]], dtype=np.float32)
    while h.shape[0] < n:
        h = np.block([[h, h], [h, -h]])
    return h.astype(np.float32)


def build_nc(dbg=False):
    nc = bass.Bass("TRN2", target_bir_lowering=False)

    def din(name, shape):
        return nc.dram_tensor(name, shape, F32, kind="ExternalInput").ap()

    x = din("x", [SEQ, DM])
    xsl = din("xsl", [TOK, DM])
    wqkv = din("wqkv", [DM, 6 * HD])
    wout = din("wout", [DM, DM])
    wup = din("wup", [INTER, DM])
    wgate = din("wgate", [INTER, DM])
    wdown = din("wdown", [DM, INTER])
    su_ug = din("su_ug", [128, NCH])
    su_dp = din("su_dp", [128, NIC])
    wsc_u = din("wsc_u", [128, NIC])
    wsc_g = din("wsc_g", [128, NIC])
    wsc_dp = din("wsc_dp", [128, NCH])
    i128 = din("i128", [128, 128])
    t128 = din("t128", [128, 128])
    h128a = din("h128a", [128, 128])
    h128b = din("h128b", [128, 128])

    out = nc.dram_tensor("out", [TOK, DM], F32, kind="ExternalOutput").ap()

    dbg_out = {}

    def scr(name, shape):
        kind = "ExternalOutput" if dbg == 1 else "Internal"
        ap = nc.dram_tensor(name, shape, F32, kind=kind).ap()
        if dbg == 2:
            dbg_out[name] = (ap, nc.dram_tensor(
                "dbg_" + name, shape, F32, kind="ExternalOutput").ap())
        return ap

    wupT = scr("wupT", [DM, INTER])
    wgT = scr("wgT", [DM, INTER])
    wdT = scr("wdT", [INTER, DM])
    qT = scr("qTs", [HPC, HD, SEQ])
    kT = scr("kTs", [HPC, HD, SEQ])
    vK = scr("vKs", [HPC, NSTRIP, 128, 512])
    rT = scr("rTs", [DM, TOK])
    a2a_i = nc.dram_tensor("a2a_i", [SEQ, HPC * HD], F32,
                           kind="Internal").ap()
    a2a_o = nc.dram_tensor("a2a_o", [SEQ, HPC * HD], F32,
                           kind="Internal").ap()
    dbg_a2a = (nc.dram_tensor("dbg_a2a", [SEQ, HPC * HD], F32,
                              kind="ExternalOutput").ap() if dbg else None)

    es = contextlib.ExitStack()
    with es:
        AR = 42000
        arena = es.enter_context(nc.sbuf_tensor("arena", [128, AR], F32))
        sb_i128 = es.enter_context(nc.sbuf_tensor("sb_i128", [128, 128], F32))
        sb_t128 = es.enter_context(nc.sbuf_tensor("sb_t128", [128, 128], F32))
        sb_h128a = es.enter_context(
            nc.sbuf_tensor("sb_h128a", [128, 128], F32))
        sb_h128b = es.enter_context(
            nc.sbuf_tensor("sb_h128b", [128, 128], F32))
        ones_c = es.enter_context(nc.sbuf_tensor("ones_c", [128, 1], F32))
        ones_r = es.enter_context(nc.sbuf_tensor("ones_r", [1, 128], F32))
        sb_su = es.enter_context(nc.sbuf_tensor("sb_su", [128, NCH], F32))
        sb_sud = es.enter_context(nc.sbuf_tensor("sb_sud", [128, NIC], F32))
        sb_wu = es.enter_context(nc.sbuf_tensor("sb_wu", [128, NIC], F32))
        sb_wg = es.enter_context(nc.sbuf_tensor("sb_wg", [128, NIC], F32))
        sb_wd = es.enter_context(nc.sbuf_tensor("sb_wd", [128, NCH], F32))
        stat = es.enter_context(nc.sbuf_tensor("stat", [128, 16], F32))
        ebias = es.enter_context(nc.sbuf_tensor("ebias", [128, 1], F32))
        cst = es.enter_context(nc.sbuf_tensor("cst", [128, 3], F32))
        rows = es.enter_context(nc.sbuf_tensor("rows", [1, 4096], F32))

        pb = [es.enter_context(nc.psum_tensor(f"pb{i}", [128, 512], F32))
              for i in range(8)]

        sems = {}
        for sname in ("te", "ve", "sc", "gc", "cc"):
            sems[sname] = es.enter_context(nc.semaphore(sname))
        for lane in range(NSY):
            sems[("sy", lane)] = es.enter_context(nc.semaphore(f"sy{lane}"))
        for lane in range(NGP):
            sems[("gp", lane)] = es.enter_context(nc.semaphore(f"gp{lane}"))

        def A(off, span):
            return arena[:, off:off + span]

        def emit(c: Ctx):
            for dst, src in ((sb_i128, i128), (sb_t128, t128),
                             (sb_h128a, h128a), (sb_h128b, h128b),
                             (sb_su, su_ug),
                             (sb_sud, su_dp), (sb_wu, wsc_u), (sb_wg, wsc_g),
                             (sb_wd, wsc_dp)):
                c.dma("sync", dst[:, :], src[:, :], "sy", w=(CONST,))
            for buf, val in ((ones_c, 1.0), (ones_r, 1.0), (ebias, EBIAS)):
                c.op("vector", lambda E, buf=buf, val=val: E.memset(
                    buf[:, :], val), r=(CONST,), w=(CONST,))
            for j, val in ((0, RD), (1, EPS), (2, -1.0)):
                c.op("vector", lambda E, j=j, val=val: E.memset(
                    cst[:, j:j + 1], val), r=(CONST,), w=(CONST,))

            xb = [A(0, 2048), A(2048, 2048)]
            stg = [A(4096, 8192), A(12288, 8192)]
            nunit = 0
            for (mi, (msrc, mdst, Rr, Cc)) in enumerate(
                    ((wup, wupT, INTER, DM), (wgate, wgT, INTER, DM),
                     (wdown, wdT, DM, INTER))):
                for rb in range(Rr // 512):
                    for cb in range(Cc // 2048):
                        u = nunit
                        nunit += 1
                        sg = stg[u % 2]
                        sgv = sg.rearrange("p (cc r) -> p cc r", r=512)
                        for rt in range(4):
                            r0 = rb * 512 + rt * 128
                            c.dma("sync", xb[rt % 2][:, :],
                                  msrc[r0:r0 + 128,
                                       cb * 2048:(cb + 1) * 2048], "sy",
                                  w=(("xb", rt % 2),))
                            for g in range(4):
                                bi = 6 + g % 2
                                c.grp([(lambda E, bi=bi, k=k, g=g, rt=rt:
                                        E.transpose(
                                            pb[bi][:, k * 128:(k + 1) * 128],
                                            xb[rt % 2][:, (4 * g + k) * 128:
                                                       (4 * g + k + 1) * 128],
                                            sb_i128[:, :]))
                                       for k in range(4)],
                                      r=(("xb", rt % 2), CONST),
                                      w=(("pb", bi),))
                                c.op("scalar", lambda E, bi=bi, g=g, rt=rt,
                                     sgv=sgv: E.copy(
                                         sgv[:, 4 * g:4 * g + 4,
                                             rt * 128:(rt + 1) * 128],
                                         pb[bi][:, :]),
                                     r=(("pb", bi),), w=(("stg", u % 2),))
                        dv = mdst.rearrange("(cc p) r -> p cc r", p=128)[
                            :, cb * 16:(cb + 1) * 16, rb * 512:(rb + 1) * 512]
                        c.dma("gpsimd", dv, sgv, "gp",
                              r=(("stg", u % 2),), w=(("wT", mi, rb, cb),))
            c.barrier()

            wq_sb = A(0, 12288)
            xt = [A(12288, 2048), A(14336, 2048)]
            xnT = A(16384, 8192)
            qkvstg = A(24576, 3072)
            vstg = A(27648, 512)
            sqscr = A(28160, 2048)
            wqv = wq_sb.rearrange("p (cc j) -> p cc j", j=768)
            c.dma("sync", wqv, wqkv.rearrange("(cc p) j -> p cc j", p=128),
                  "sy", w=(("wq",),))
            xnv = xnT.rearrange("p (cc t) -> p cc t", t=512)
            for s in range(NSTRIP):
                for tt in range(4):
                    t0 = s * 512 + tt * 128
                    XT = ("xt", tt % 2)
                    c.dma("sync", xt[tt % 2][:, :], x[t0:t0 + 128, :], "sy",
                          w=(XT,))
                    c.op("vector", lambda E, tt=tt: E.reduce_sum(
                        stat[:, 0:1], xt[tt % 2][:, :], axis=AX),
                        r=(XT,), w=(("st", 0),))
                    c.op("vector", lambda E: E.tensor_scalar(
                        stat[:, 1:2], stat[:, 0:1], cst[:, 0:1], None, MUL),
                        r=(("st", 0), CONST), w=(("st", 1),))
                    c.op("vector", lambda E: E.tensor_mul(
                        stat[:, 2:3], stat[:, 1:2], stat[:, 1:2]),
                        r=(("st", 1),), w=(("st", 2),))
                    c.op("vector", lambda E: E.tensor_scalar(
                        stat[:, 3:4], stat[:, 2:3], cst[:, 2:3], cst[:, 1:2],
                        MUL, ADD), r=(("st", 2), CONST), w=(("st", 3),))
                    c.op("scalar", lambda E, tt=tt: E.activation(
                        sqscr[:, :], xt[tt % 2][:, :], AF.Square,
                        accum_out=stat[:, 4:5]),
                        r=(XT,), w=(("sqscr",), ("st", 4)))
                    c.op("scalar", lambda E: E.activation(
                        stat[:, 5:6], stat[:, 4:5], AF.Sqrt,
                        bias=stat[:, 3:4], scale=RD),
                        r=(("st", 4), ("st", 3)), w=(("st", 5),))
                    c.op("vector", lambda E: E.reciprocal(
                        stat[:, 6:7], stat[:, 5:6]),
                        r=(("st", 5),), w=(("st", 6),))
                    c.op("vector", lambda E, tt=tt: E.tensor_scalar(
                        xt[tt % 2][:, :], xt[tt % 2][:, :],
                        stat[:, 1:2], stat[:, 6:7], SUB, MUL),
                        r=(XT, ("st", 1), ("st", 6)), w=(XT,))
                    for g in range(4):
                        bi = 6 + g % 2
                        c.grp([(lambda E, bi=bi, k=k, g=g, tt=tt: E.transpose(
                            pb[bi][:, k * 128:(k + 1) * 128],
                            xt[tt % 2][:, (4 * g + k) * 128:
                                       (4 * g + k + 1) * 128],
                            sb_i128[:, :])) for k in range(4)],
                            r=(XT, CONST), w=(("pb", bi),))
                        c.op("scalar", lambda E, bi=bi, g=g, tt=tt: E.copy(
                            xnv[:, 4 * g:4 * g + 4, tt * 128:(tt + 1) * 128],
                            pb[bi][:, :]),
                            r=(("pb", bi),), w=(("xnT", g, tt),))
                xnkeys = tuple(("xnT", g, tt) for g in range(4)
                               for tt in range(4))
                for jt in range(6):
                    c.grp([(lambda E, jt=jt, ccx=ccx: E.matmul(
                        pb[jt][:, 0:512],
                        wqv[:, ccx, jt * 128:(jt + 1) * 128],
                        xnv[:, ccx, :], start=(ccx == 0),
                        stop=(ccx == NCH - 1))) for ccx in range(NCH)],
                        r=xnkeys + (("wq",),), w=(("pb", jt),))
                for jt in range(6):
                    fn = (lambda E, jt=jt: E.mul(
                        qkvstg[:, jt * 512:(jt + 1) * 512], pb[jt][:, 0:512],
                        QSC)) if jt < 2 else (lambda E, jt=jt: E.copy(
                            qkvstg[:, jt * 512:(jt + 1) * 512],
                            pb[jt][:, 0:512]))
                    c.op("scalar", fn, r=(("pb", jt),), w=(("qkvstg", jt),))
                for h in range(2):
                    bi = 6 + h
                    c.grp([(lambda E, bi=bi, k=k, h=h: E.transpose(
                        pb[bi][:, k * 128:(k + 1) * 128],
                        qkvstg[:, (4 + h) * 512 + k * 128:
                               (4 + h) * 512 + (k + 1) * 128],
                        sb_i128[:, :])) for k in range(4)],
                        r=(("qkvstg", 4 + h), CONST), w=(("pb", bi),))
                    c.op("vector", lambda E, bi=bi: E.tensor_copy(
                        vstg[:, :], pb[bi][:, :]),
                        r=(("pb", bi),), w=(("vstg",),))
                    c.dma("gpsimd", vK[h, s, :, :], vstg[:, :], "gp",
                          r=(("vstg",),), w=(("vK", h, s),))
                for h in range(2):
                    c.dma("gpsimd", qT[h, :, s * 512:(s + 1) * 512],
                          qkvstg[:, h * 512:(h + 1) * 512], "gp",
                          r=(("qkvstg", h),), w=(("qT", h, s),))
                    c.dma("gpsimd", kT[h, :, s * 512:(s + 1) * 512],
                          qkvstg[:, (2 + h) * 512:(3 + h) * 512], "gp",
                          r=(("qkvstg", 2 + h),), w=(("kT", h, s),))
            c.barrier()

            ktb = A(0, 8192)
            vaug = A(8192, 8256)
            qb = [A(16448, 512), A(16960, 512)]
            pbf = [A(17472, 512), A(17984, 512)]
            ostg = [A(18496, 128), A(18624, 128)]
            vaugv = vaug.rearrange("p (k e) -> p k e", e=129)
            for h in range(2):
                kkeys = tuple(("kT", h, s2) for s2 in range(NSTRIP))
                vkeys = tuple(("vK", h, s2) for s2 in range(NSTRIP))
                c.dma("sync", ktb[:, :], kT[h, :, :], "sy",
                      r=kkeys, w=(("ktb",),))
                for s2 in range(NSTRIP):
                    c.dma("sync", vaugv[:, s2 * 4:(s2 + 1) * 4, 0:128],
                          vK[h, s2].rearrange("p (kb d) -> p kb d", d=128),
                          "sy", r=(("vK", h, s2),), w=(("vaug",),))
                c.op("vector", lambda E: E.memset(vaugv[:, :, 128:129], 1.0),
                     r=(), w=(("vaug1",),))
                for s in range(NSTRIP):
                    c.dma("sync", qb[s % 2][:, :],
                          qT[h, :, s * 512:(s + 1) * 512], "sy",
                          r=(("qT", h, s),), w=(("qb", s % 2),))
                    for kc in range(4 * s + 4):
                        diag = kc - 4 * s
                        sbk = kc % 2
                        c.grp([lambda E, sbk=sbk, kc=kc, s=s: E.matmul(
                            pb[sbk][:, 0:512],
                            ktb[:, kc * 128:(kc + 1) * 128],
                            qb[s % 2][:, :], start=True, stop=True)],
                            r=(("ktb",), ("qb", s % 2)), w=(("pb", sbk),))
                        o0 = 128 * diag if diag >= 0 else 0
                        c.op("scalar", lambda E, sbk=sbk, kc=kc, o0=o0:
                             E.activation(pbf[kc % 2][:, o0:512],
                                          pb[sbk][:, o0:512],
                                          AF.Exp, bias=ebias[:, 0:1]),
                             r=(("pb", sbk), CONST), w=(("pbf", kc % 2),))
                        if diag >= 0:
                            c.op("vector", lambda E, kc=kc, o0=o0:
                                 E.tensor_mul(
                                     pbf[kc % 2][:, o0:o0 + 128],
                                     pbf[kc % 2][:, o0:o0 + 128],
                                     sb_t128[:, :]),
                                 r=(("pbf", kc % 2), CONST),
                                 w=(("pbf", kc % 2),))
                        t_lo = max(diag, 0)
                        for t in range(t_lo, 4):
                            stop = (kc == 4 * s + t)
                            c.grp([lambda E, t=t, kc=kc, stop=stop: E.matmul(
                                pb[2 + t][:, 0:129],
                                pbf[kc % 2][:, t * 128:(t + 1) * 128],
                                vaugv[:, kc, :],
                                start=(kc == 0), stop=stop)],
                                r=(("pbf", kc % 2), ("vaug",), ("vaug1",)),
                                w=(("pb", 2 + t),) if kc == 0 else
                                (("pbacc", 2 + t),))
                    for t in range(4):
                        c.op("vector", lambda E, t=t: E.reciprocal(
                            stat[:, 8:9], pb[2 + t][:, 128:129]),
                            r=(("pb", 2 + t), ("pbacc", 2 + t)),
                            w=(("st", 8),))
                        c.op("vector", lambda E, t=t: E.tensor_scalar(
                            ostg[t % 2][:, :], pb[2 + t][:, 0:128],
                            stat[:, 8:9], None, MUL),
                            r=(("pb", 2 + t), ("pbacc", 2 + t), ("st", 8)),
                            w=(("ostg", t % 2),))
                        c.dma("gpsimd",
                              a2a_i[s * 512 + t * 128:s * 512 + (t + 1) * 128,
                                    h * 128:(h + 1) * 128],
                              ostg[t % 2][:, :], "gp",
                              r=(("ostg", t % 2),), w=(("a2a_i", h, s, t),))

            if c.eng == "gpsimd":
                c.wq("gp")
            with allow_ici_all_to_all():
                if c.eng == "gpsimd":
                    c.E.collective_compute(
                        "AllToAll", mybir.AluOpType.bypass,
                        replica_groups=[list(range(NCORE))],
                        ins=[a2a_i.opt()], outs=[a2a_o.opt()],
                    ).then_inc(c.sems["cc"], 1)
            c.cnt["cc"] += 1
            c._fin(("cc", c.cnt["cc"], "gpsimd"), (), (("a2a_o",),))
            if dbg:
                c.dma("gpsimd", dbg_a2a[:, :], a2a_o[:, :], "gp",
                      r=(("a2a_o",),), w=())
            c.barrier()

            attnT = A(0, 16384)
            yT = A(16384, 16384)
            wob = [A(32768, 2048), A(34816, 2048)]
            abuf = [A(36864, 128), A(36992, 128)]
            sq2 = A(37120, 1024)
            xt2 = A(38144, 2048)
            attv = attnT.rearrange("p (cc t) -> p cc t", t=1024)
            for ch in range(NCH):
                cb, col0 = ch // 2, (ch % 2) * 128
                for g in range(2):
                    bi = g % 2
                    for k in range(4):
                        tt = g * 4 + k
                        c.dma("sync", abuf[tt % 2][:, :],
                              a2a_o[cb * 1024 + tt * 128:
                                    cb * 1024 + (tt + 1) * 128,
                                    col0:col0 + 128], "sy",
                              r=(("a2a_o",),), w=(("abuf", tt % 2),))
                        c.grp([lambda E, bi=bi, k=k, tt=tt: E.transpose(
                            pb[bi][:, k * 128:(k + 1) * 128],
                            abuf[tt % 2][:, :], sb_i128[:, :])],
                            r=(("abuf", tt % 2), CONST), w=(("pb", bi),))
                    c.op("scalar", lambda E, bi=bi, ch=ch, g=g: E.copy(
                        attv[:, ch, g * 512:(g + 1) * 512], pb[bi][:, :]),
                        r=(("pb", bi),), w=(("attnT", ch),))

            woutv = wout.rearrange("(cc p) j -> p cc j", p=128)
            akeys = tuple(("attnT", ch) for ch in range(NCH))
            for jt in range(NCH):
                c.dma("sync",
                      wob[jt % 2].rearrange("p (cc j) -> p cc j", j=128),
                      woutv[:, :, jt * 128:(jt + 1) * 128], "sy",
                      w=(("wob", jt % 2),))
                wv = wob[jt % 2].rearrange("p (cc j) -> p cc j", j=128)
                for half in range(2):
                    bi = 2 * (jt % 2) + half
                    c.grp([(lambda E, bi=bi, wv=wv, ccx=ccx, half=half:
                            E.matmul(
                                pb[bi][:, 0:512], wv[:, ccx, :],
                                attv[:, ccx, half * 512:(half + 1) * 512],
                                start=(ccx == 0), stop=(ccx == NCH - 1)))
                           for ccx in range(NCH)],
                        r=akeys + (("wob", jt % 2),), w=(("pb", bi),))
                    c.op("scalar", lambda E, bi=bi, jt=jt, half=half: E.copy(
                        yT[:, jt * 1024 + half * 512:
                           jt * 1024 + (half + 1) * 512], pb[bi][:, :]),
                        r=(("pb", bi),), w=(("yT", jt),))

            def fm_ln(zread_fn, nch, width, sqbuf, sqkey, rowoff, banks,
                      apply_fn):
                b0, b1, b2, b3 = banks
                for ccx in range(nch):
                    ap, rk = zread_fn(ccx)
                    c.op("scalar", lambda E, ap=ap: E.activation(
                        sqbuf, ap, AF.Square), r=rk, w=(sqkey,))
                    c.grp([lambda E, ap=ap, ccx=ccx: E.matmul(
                        pb[b0][0:1, 0:width], ones_c[:, :], ap,
                        start=(ccx == 0), stop=(ccx == nch - 1))],
                        r=rk + (CONST,),
                        w=((("pb", b0),) if ccx == 0 else (("pbacc", b0),)))
                    c.grp([lambda E, ccx=ccx: E.matmul(
                        pb[b1][0:1, 0:width], ones_c[:, :], sqbuf,
                        start=(ccx == 0), stop=(ccx == nch - 1))],
                        r=(sqkey, CONST),
                        w=((("pb", b1),) if ccx == 0 else (("pbacc", b1),)))
                ro = rows[0:1, rowoff:rowoff + width]
                ro2 = rows[0:1, rowoff + width:rowoff + 2 * width]
                RO, RO2 = ("ro", rowoff), ("ro2", rowoff)
                PB0 = (("pb", b0), ("pbacc", b0))
                PB1 = (("pb", b1), ("pbacc", b1))
                c.op("vector", lambda E: E.tensor_scalar(
                    ro, pb[b0][0:1, 0:width], cst[0:1, 0:1], None, MUL),
                    r=PB0 + (CONST,), w=(RO,))
                c.op("vector", lambda E: E.tensor_scalar(
                    ro2, pb[b1][0:1, 0:width], cst[0:1, 0:1], None, MUL),
                    r=PB1 + (CONST,), w=(RO2,))
                c.op("vector", lambda E: E.tensor_mul(
                    pb[b0][0:1, 0:width], ro, ro),
                    r=(RO,) + PB0, w=(("pb", b0),))
                c.op("vector", lambda E: E.tensor_sub(
                    ro2, ro2, pb[b0][0:1, 0:width]),
                    r=(RO2, ("pb", b0)), w=(RO2,))
                c.op("vector", lambda E: E.tensor_scalar(
                    ro2, ro2, cst[0:1, 1:2], None, ADD),
                    r=(RO2, CONST), w=(RO2,))
                c.op("scalar", lambda E: E.activation(ro2, ro2, AF.Sqrt),
                     r=(RO2,), w=(RO2,))
                c.op("vector", lambda E: E.reciprocal(ro2, ro2),
                     r=(RO2,), w=(RO2,))
                c.op("vector", lambda E: E.tensor_mul(ro, ro, ro2),
                     r=(RO, RO2), w=(RO,))
                c.grp([lambda E: E.matmul(
                    pb[b2][:, 0:width], ones_r[0:1, :], ro2,
                    start=True, stop=True)], r=(RO2, CONST), w=(("pb", b2),))
                c.grp([lambda E: E.matmul(
                    pb[b3][:, 0:width], ones_r[0:1, :], ro,
                    start=True, stop=True)], r=(RO, CONST), w=(("pb", b3),))
                for ccx in range(nch):
                    apply_fn(ccx, pb[b2][:, 0:width], pb[b3][:, 0:width],
                             (("pb", b2), ("pb", b3)))

            for half in range(2):
                hoff = half * 512

                def yslice(ccx, hoff=hoff):
                    return yT[:, ccx * 1024 + hoff:ccx * 1024 + hoff + 512]

                def ap5(ccx, s_ap, ms_ap, bk):
                    c.op("vector", lambda E, ccx=ccx, s_ap=s_ap: E.tensor_mul(
                        yslice(ccx), yslice(ccx), s_ap),
                        r=(("yT", ccx),) + bk, w=(("yT", ccx),))
                    c.op("vector", lambda E, ccx=ccx, ms_ap=ms_ap:
                         E.tensor_sub(yslice(ccx), yslice(ccx), ms_ap),
                         r=(("yT", ccx),) + bk, w=(("yT", ccx),))
                fm_ln(lambda ccx: (yslice(ccx), (("yT", ccx),)),
                      NCH, 512, sq2[:, 0:512], ("sq2",),
                      1024 * half, (4, 5, 6, 7), ap5)
            yv1 = yT.rearrange("p (cc t) -> p cc t", t=1024)
            for tt in range(8):
                c.dma("sync", xt2[:, :], xsl[tt * 128:(tt + 1) * 128, :],
                      "sy", w=(("xt2",),))
                for g in range(4):
                    bi = g % 2
                    c.grp([(lambda E, bi=bi, k=k, g=g: E.transpose(
                        pb[bi][:, k * 128:(k + 1) * 128],
                        xt2[:, (4 * g + k) * 128:(4 * g + k + 1) * 128],
                        sb_i128[:, :])) for k in range(4)],
                        r=(("xt2",), CONST), w=(("pb", bi),))
                    ykeys = tuple(("yT", 4 * g + k) for k in range(4))
                    c.op("vector", lambda E, bi=bi, g=g, tt=tt:
                         E.tensor_tensor(
                             yv1[:, 4 * g:4 * g + 4, tt * 128:(tt + 1) * 128],
                             yv1[:, 4 * g:4 * g + 4, tt * 128:(tt + 1) * 128],
                             pb[bi][:, :], ADD),
                         r=ykeys + (("pb", bi),), w=ykeys)
            c.dma("gpsimd", rT.rearrange("(cc p) t -> p cc t", p=128),
                  yT.rearrange("p (cc t) -> p cc t", t=1024), "gp",
                  r=tuple(("yT", j) for j in range(NCH)), w=(("rT",),))
            c.barrier()

            rsl = A(0, 4096)
            zb = A(4096, 4096)
            h2 = A(8192, 16384)
            wb = [A(24576, 2048), A(26624, 2048),
                  A(28672, 2048), A(30720, 2048)]
            hh = A(32768, 256)
            sq6 = A(33024, 256)
            tmp6 = A(33280, 256)
            ostg2 = A(33536, 2048)
            z1 = A(35584, 4096)
            rslv = rsl.rearrange("p (cc t) -> p cc t", t=FT)
            zv = zb.rearrange("p (cc t) -> p cc t", t=FT)
            z1v = z1.rearrange("p (cc t) -> p cc t", t=FT)
            y2v = zb.rearrange("p (cc t) -> p cc t", t=FT)
            h2v = h2.rearrange("p (ic t) -> p ic t", t=FT)
            wupTv = wupT.rearrange("(cc p) i -> p cc i", p=128)
            wgTv = wgT.rearrange("(cc p) i -> p cc i", p=128)
            wdTv = wdT.rearrange("(ic p) j -> p ic j", p=128)
            rTv = rT.rearrange("(cc p) t -> p cc t", p=128)
            zbkeys = tuple(("zb", ccx) for ccx in range(NCH))

            for sub in range(NSUB):
                toff = sub * FT
                c.dma("sync", rslv, rTv[:, :, toff:toff + FT], "sy",
                      r=(("rT",),), w=(("rsl",),))

                def apz(ccx, s_ap, ms_ap, bk):
                    c.op("vector", lambda E, ccx=ccx, s_ap=s_ap:
                         E.tensor_tensor(zv[:, ccx, :], rslv[:, ccx, :],
                                         s_ap, MUL),
                         r=(("rsl",),) + bk, w=(("zb", ccx),))
                    c.op("vector", lambda E, ccx=ccx, ms_ap=ms_ap:
                         E.tensor_sub(zv[:, ccx, :], zv[:, ccx, :], ms_ap),
                         r=(("zb", ccx),) + bk, w=(("zb", ccx),))
                    c.op("vector", lambda E, ccx=ccx: E.tensor_scalar(
                        zv[:, ccx, :], zv[:, ccx, :], sb_su[:, ccx:ccx + 1],
                        None, MUL), r=(("zb", ccx), CONST), w=(("zb", ccx),))
                fm_ln(lambda ccx: (rslv[:, ccx, :], (("rsl",),)),
                      NCH, FT, sq6[:, :], ("sq6",), 2048, (0, 1, 2, 3), apz)
                for ccx in range(NCH):
                    bi = 4 + ccx % 2
                    c.grp([lambda E, bi=bi, ccx=ccx: E.matmul(
                        pb[bi][:, 0:FT], sb_h128a[:, :], zv[:, ccx, :],
                        start=True, stop=True)],
                        r=(("zb", ccx), CONST), w=(("pb", bi),))
                    c.op("scalar", lambda E, bi=bi, ccx=ccx: E.copy(
                        z1v[:, ccx, :], pb[bi][:, 0:FT]),
                        r=(("pb", bi),), w=(("z1", ccx),))
                for st in range(4):
                    bit = 1 << st
                    for i in range(NCH):
                        if i & bit:
                            continue
                        j = i | bit
                        c.op("vector", lambda E, i=i, j=j: E.tensor_sub(
                            tmp6[:, :], z1v[:, i, :], z1v[:, j, :]),
                            r=(("z1", i), ("z1", j)), w=(("tmp6",),))
                        c.op("vector", lambda E, i=i, j=j: E.tensor_add(
                            z1v[:, i, :], z1v[:, i, :], z1v[:, j, :]),
                            r=(("z1", i), ("z1", j)), w=(("z1", i),))
                        c.op("vector", lambda E, j=j: E.tensor_copy(
                            z1v[:, j, :], tmp6[:, :]),
                            r=(("tmp6",),), w=(("z1", j),))
                z1keys = tuple(("z1", ccx) for ccx in range(NCH))
                for ic in range(NIC):
                    c.dma("sync",
                          wb[ic % 2].rearrange("p (cc j) -> p cc j", j=128),
                          wupTv[:, :, ic * 128:(ic + 1) * 128], "sy",
                          r=(("wT", 0, ic // 4, 0),), w=(("wb", ic % 2),))
                    c.dma("sync",
                          wb[2 + ic % 2].rearrange("p (cc j) -> p cc j",
                                                   j=128),
                          wgTv[:, :, ic * 128:(ic + 1) * 128], "sy",
                          r=(("wT", 1, ic // 4, 0),), w=(("wb", 2 + ic % 2),))
                    buv = wb[ic % 2].rearrange("p (cc j) -> p cc j", j=128)
                    bgv = wb[2 + ic % 2].rearrange("p (cc j) -> p cc j",
                                                   j=128)
                    bu, bg = ic % 2, 2 + ic % 2
                    c.grp([(lambda E, bu=bu, buv=buv, ccx=ccx: E.matmul(
                        pb[bu][:, 0:FT], buv[:, ccx, :], z1v[:, ccx, :],
                        start=(ccx == 0), stop=(ccx == NCH - 1)))
                        for ccx in range(NCH)],
                        r=z1keys + (("wb", bu),), w=(("pb", bu),))
                    c.grp([(lambda E, bg=bg, bgv=bgv, ccx=ccx: E.matmul(
                        pb[bg][:, 0:FT], bgv[:, ccx, :], z1v[:, ccx, :],
                        start=(ccx == 0), stop=(ccx == NCH - 1)))
                        for ccx in range(NCH)],
                        r=z1keys + (("wb", bg),), w=(("pb", bg),))
                    c.op("scalar", lambda E, bg=bg, ic=ic: E.activation(
                        sq6[:, :], pb[bg][:, 0:FT], AF.Silu,
                        scale=sb_wg[:, ic:ic + 1]),
                        r=(("pb", bg), CONST), w=(("sq6",),))
                    c.op("vector", lambda E, bu=bu, ic=ic: E.tensor_scalar(
                        hh[:, :], pb[bu][:, 0:FT], sb_wu[:, ic:ic + 1],
                        sb_sud[:, ic:ic + 1], MUL, MUL),
                        r=(("pb", bu), CONST), w=(("hh",),))
                    c.op("vector", lambda E: E.tensor_mul(
                        hh[:, :], hh[:, :], sq6[:, :]),
                        r=(("hh",), ("sq6",)), w=(("hh",),))
                    bh = 4 + ic % 2
                    c.grp([lambda E, bh=bh: E.matmul(
                        pb[bh][:, 0:FT], sb_h128b[:, :], hh[:, :],
                        start=True, stop=True)],
                        r=(("hh",), CONST), w=(("pb", bh),))
                    c.op("scalar", lambda E, bh=bh, ic=ic: E.copy(
                        h2v[:, ic, :], pb[bh][:, 0:FT]),
                        r=(("pb", bh),), w=(("h2", ic),))
                for st in range(6):
                    bit = 1 << st
                    for i in range(NIC):
                        if i & bit:
                            continue
                        j = i | bit
                        c.op("vector", lambda E, i=i, j=j: E.tensor_sub(
                            tmp6[:, :], h2v[:, i, :], h2v[:, j, :]),
                            r=(("h2", i), ("h2", j)), w=(("tmp6",),))
                        c.op("vector", lambda E, i=i, j=j: E.tensor_add(
                            h2v[:, i, :], h2v[:, i, :], h2v[:, j, :]),
                            r=(("h2", i), ("h2", j)), w=(("h2", i),))
                        c.op("vector", lambda E, j=j: E.tensor_copy(
                            h2v[:, j, :], tmp6[:, :]),
                            r=(("tmp6",),), w=(("h2", j),))
                h2keys = tuple(("h2", ic) for ic in range(NIC))
                for jt in range(NCH):
                    bd = 6 + jt % 2
                    for q in range(4):
                        c.dma("sync",
                              wb[q % 2].rearrange("p (ic j) -> p ic j",
                                                  j=128),
                              wdTv[:, q * 16:(q + 1) * 16,
                                   jt * 128:(jt + 1) * 128], "sy",
                              r=(("wT", 2, jt // 4, q),),
                              w=(("wb", q % 2),))
                        wdv = wb[q % 2].rearrange("p (ic j) -> p ic j", j=128)
                        c.grp([(lambda E, bd=bd, wdv=wdv, icq=icq, q=q:
                                E.matmul(
                                    pb[bd][:, 0:FT], wdv[:, icq, :],
                                    h2v[:, q * 16 + icq, :],
                                    start=(q * 16 + icq == 0),
                                    stop=(q * 16 + icq == NIC - 1)))
                               for icq in range(16)],
                            r=h2keys + (("wb", q % 2),),
                            w=((("pb", bd),) if q == 0 else
                               (("pbacc", bd),)))
                    c.op("vector", lambda E, jt=jt, bd=bd: E.tensor_scalar(
                        y2v[:, jt, :], pb[bd][:, 0:FT],
                        sb_wd[:, jt:jt + 1], None, MUL),
                        r=(("pb", bd), ("pbacc", bd), CONST),
                        w=(("zb", jt),))

                def apo(ccx, s_ap, ms_ap, bk):
                    c.op("vector", lambda E, ccx=ccx, s_ap=s_ap: E.tensor_mul(
                        y2v[:, ccx, :], y2v[:, ccx, :], s_ap),
                        r=(("zb", ccx),) + bk, w=(("zb", ccx),))
                    c.op("vector", lambda E, ccx=ccx, ms_ap=ms_ap:
                         E.tensor_sub(y2v[:, ccx, :], y2v[:, ccx, :], ms_ap),
                         r=(("zb", ccx),) + bk, w=(("zb", ccx),))
                    c.op("vector", lambda E, ccx=ccx: E.tensor_tensor(
                        y2v[:, ccx, :], y2v[:, ccx, :], rslv[:, ccx, :],
                        ADD), r=(("zb", ccx), ("rsl",)), w=(("zb", ccx),))
                fm_ln(lambda ccx: (y2v[:, ccx, :], (("zb", ccx),)),
                      NCH, FT, sq6[:, :], ("sq6",), 2048, (0, 1, 2, 3), apo)
                for tt2 in range(2):
                    for g in range(4):
                        bi = 4 + g % 2
                        zkeys = tuple(("zb", 4 * g + k) for k in range(4))
                        c.grp([(lambda E, bi=bi, k=k, g=g, tt2=tt2:
                                E.transpose(
                                    pb[bi][:, k * 128:(k + 1) * 128],
                                    y2v[:, 4 * g + k,
                                        tt2 * 128:(tt2 + 1) * 128],
                                    sb_i128[:, :])) for k in range(4)],
                            r=zkeys + (CONST,), w=(("pb", bi),))
                        c.op("scalar", lambda E, bi=bi, g=g: E.copy(
                            ostg2[:, g * 512:(g + 1) * 512], pb[bi][:, :]),
                            r=(("pb", bi),), w=(("ostg2", g),))
                    c.dma("gpsimd",
                          out[toff + tt2 * 128:toff + (tt2 + 1) * 128, :],
                          ostg2[:, :], "gp",
                          r=tuple(("ostg2", g) for g in range(4)), w=())
            if dbg == 2:
                c.barrier()
                for name, (src, dst) in dbg_out.items():
                    nd = len(src.tensor.shape)
                    sl = (slice(None),) * nd
                    c.dma("gpsimd", dst[sl], src[sl], "gp")
            if c.eng == "gpsimd":
                c.wq("gp")

        with nc.Block() as block:
            @block.sync
            def _(e):
                emit(Ctx("sync", e, sems))

            @block.tensor
            def _(e):
                emit(Ctx("tensor", e, sems))

            @block.vector
            def _(e):
                emit(Ctx("vector", e, sems))

            @block.scalar
            def _(e):
                emit(Ctx("scalar", e, sems))

            @block.gpsimd
            def _(e):
                emit(Ctx("gpsimd", e, sems))

    return nc


def kernel(L0_attn_x, L0_attn_w_qkv, L0_attn_w_out, L0_ffn_SU_ug, L0_ffn_SU_dp,
           L0_ffn_Wscale_ug, L0_ffn_Wscale_dp, L0_ffn_W_up, L0_ffn_W_gate,
           L0_ffn_W_down):
    cont = np.ascontiguousarray
    x = cont(np.asarray(L0_attn_x, dtype=np.float32).reshape(SEQ, DM))
    wqkv = np.asarray(L0_attn_w_qkv, dtype=np.float32)
    base = {
        "x": x,
        "wout": cont(np.asarray(L0_attn_w_out, dtype=np.float32)),
        "wup": cont(np.asarray(L0_ffn_W_up, dtype=np.float32)),
        "wgate": cont(np.asarray(L0_ffn_W_gate, dtype=np.float32)),
        "wdown": cont(np.asarray(L0_ffn_W_down, dtype=np.float32)),
        "su_ug": cont(np.asarray(L0_ffn_SU_ug, dtype=np.float32)
                      .reshape(NCH, 128).T),
        "su_dp": cont(np.asarray(L0_ffn_SU_dp, dtype=np.float32)
                      .reshape(NIC, 128).T),
        "wsc_u": cont(np.asarray(L0_ffn_Wscale_ug, dtype=np.float32)[:INTER]
                      .reshape(NIC, 128).T),
        "wsc_g": cont(np.asarray(L0_ffn_Wscale_ug, dtype=np.float32)[INTER:]
                      .reshape(NIC, 128).T),
        "wsc_dp": cont(np.asarray(L0_ffn_Wscale_dp, dtype=np.float32)
                       .reshape(NCH, 128).T),
        "i128": np.eye(128, dtype=np.float32),
        "t128": (np.arange(128)[None, :] >=
                 np.arange(128)[:, None]).astype(np.float32),
        "h128a": hadamard(128) / np.float32(np.sqrt(2048.0)),
        "h128b": hadamard(128) / np.float32(np.sqrt(8192.0)),
    }
    nc = build_nc()
    in_maps = []
    for cidx in range(NCORE):
        h0, h1 = 2 * cidx, 2 * cidx + 1
        wq_s = cont(np.concatenate(
            [wqkv[:, h0 * HD:(h0 + 1) * HD],
             wqkv[:, h1 * HD:(h1 + 1) * HD],
             wqkv[:, DM + h0 * HD:DM + (h0 + 1) * HD],
             wqkv[:, DM + h1 * HD:DM + (h1 + 1) * HD],
             wqkv[:, 2 * DM + h0 * HD:2 * DM + (h0 + 1) * HD],
             wqkv[:, 2 * DM + h1 * HD:2 * DM + (h1 + 1) * HD]], axis=1))
        m = dict(base)
        m["wqkv"] = wq_s
        m["xsl"] = cont(x[cidx * TOK:(cidx + 1) * TOK, :])
        in_maps.append(m)
    res = run_bass_kernel_spmd(nc, in_maps, core_ids=list(range(NCORE)))
    outp = np.concatenate([res.results[cidx]["out"] for cidx in range(NCORE)],
                          axis=0)
    return outp.reshape(1, SEQ, DM)


# version 05
# speedup vs baseline: 1.1170x; 1.1170x over previous
import contextlib
import numpy as np

import concourse.bass as bass
import concourse.mybir as mybir
from concourse.bass_utils import run_bass_kernel_spmd
from concourse.aot_env import allow_ici_all_to_all

F32 = mybir.dt.float32
AX = mybir.AxisListType.X
ADD = mybir.AluOpType.add
SUB = mybir.AluOpType.subtract
MUL = mybir.AluOpType.mult
AF = mybir.ActivationFunctionType

SEQ, DM, NH, HD, NCORE = 8192, 2048, 16, 128, 8
HPC = NH // NCORE
TOK = SEQ // NCORE
INTER = 8192
QS = 512
NSTRIP = SEQ // QS
FT = 256
NSUB = TOK // FT
NCH = DM // 128
NIC = INTER // 128
EPS = 1e-5
EBIAS = -20.0
QSC = float(1.0 / np.sqrt(128.0))
RD = float(1.0 / DM)

OWN = {"tensor": "te", "vector": "ve", "scalar": "sc", "gpsimd": "gc"}
NSY = 8
NGP = 4
CONST = ("const",)


class Ctx:

    def __init__(self, eng, E, sems):
        self.eng = eng
        self.E = E
        self.sems = sems
        self.cnt = {k: 0 for k in sems}
        self._lastw = {}
        self._rr = {"sy": 0, "gp": 0}
        self.last_w = {}
        self.readers = {}

    def w(self, sem, val):
        if val is None or val <= 0:
            return
        if self._lastw.get(sem, -1) >= val:
            return
        self._lastw[sem] = val
        if self.E is not None:
            self.E.wait_ge(self.sems[sem], val)

    def _dep(self, eng, reads, writes):
        emitting = (self.eng == eng)
        for k in reads:
            t = self.last_w.get(k)
            if t is not None and not (eng == "tensor" and t[2] == "tensor"):
                if emitting:
                    self.w(t[0], t[1])
        for k in writes:
            t = self.last_w.get(k)
            if t is not None and not (eng == "tensor" and t[2] == "tensor"):
                if emitting:
                    self.w(t[0], t[1])
            for rt in self.readers.get(k, ()):
                if eng == "tensor" and rt[2] == "tensor":
                    continue
                if emitting:
                    self.w(rt[0], rt[1])

    def _fin(self, tok, reads, writes):
        for k in writes:
            self.last_w[k] = tok
            self.readers[k] = []
        for k in reads:
            self.readers.setdefault(k, []).append(tok)

    def op(self, eng, fn, r=(), w=()):
        self._dep(eng, r, w)
        sem = OWN[eng]
        if self.eng == eng:
            fn(self.E).then_inc(self.sems[sem], 1)
        self.cnt[sem] += 1
        tok = (sem, self.cnt[sem], eng)
        self._fin(tok, r, w)
        return tok

    def grp(self, fns, r=(), w=()):
        self._dep("tensor", r, w)
        n = len(fns)
        if self.eng == "tensor":
            for i, fn in enumerate(fns):
                ins = fn(self.E)
                if i == n - 1:
                    ins.then_inc(self.sems["te"], 1)
        self.cnt["te"] += 1
        tok = ("te", self.cnt["te"], "tensor")
        self._fin(tok, r, w)
        return tok

    def dma(self, eng, out, in_, q, r=(), w=()):
        self._dep(eng, r, w)
        n = NSY if q == "sy" else NGP
        lane = self._rr[q]
        self._rr[q] = (lane + 1) % n
        sem = (q, lane)
        if self.eng == eng:
            self.w(sem, self.cnt[sem])
            self.E.dma_start(out=out, in_=in_).then_inc(self.sems[sem], 16)
        self.cnt[sem] += 16
        tok = (sem, self.cnt[sem], eng)
        self._fin(tok, r, w)
        return tok

    def wq(self, q):
        n = NSY if q == "sy" else NGP
        for lane in range(n):
            self.w((q, lane), self.cnt[(q, lane)])

    def barrier(self):
        for sem in ("te", "ve", "sc", "gc", "cc"):
            self.w(sem, self.cnt[sem])
        self.wq("sy")
        self.wq("gp")


def hadamard(n):
    h = np.array([[1.0]], dtype=np.float32)
    while h.shape[0] < n:
        h = np.block([[h, h], [h, -h]])
    return h.astype(np.float32)


def build_nc(dbg=False):
    nc = bass.Bass("TRN2", target_bir_lowering=False)

    def din(name, shape):
        return nc.dram_tensor(name, shape, F32, kind="ExternalInput").ap()

    x = din("x", [SEQ, DM])
    xsl = din("xsl", [TOK, DM])
    wqkv = din("wqkv", [DM, 6 * HD])
    wout = din("wout", [DM, DM])
    wup = din("wup", [INTER, DM])
    wgate = din("wgate", [INTER, DM])
    wdown = din("wdown", [DM, INTER])
    su_ug = din("su_ug", [128, NCH])
    su_dp = din("su_dp", [128, NIC])
    wsc_u = din("wsc_u", [128, NIC])
    wsc_g = din("wsc_g", [128, NIC])
    wsc_dp = din("wsc_dp", [128, NCH])
    i128 = din("i128", [128, 128])
    t128 = din("t128", [128, 128])
    h128a = din("h128a", [128, 128])
    h128b = din("h128b", [128, 128])

    out = nc.dram_tensor("out", [TOK, DM], F32, kind="ExternalOutput").ap()

    dbg_out = {}

    def scr(name, shape):
        kind = "ExternalOutput" if dbg == 1 else "Internal"
        ap = nc.dram_tensor(name, shape, F32, kind=kind).ap()
        if dbg == 2:
            dbg_out[name] = (ap, nc.dram_tensor(
                "dbg_" + name, shape, F32, kind="ExternalOutput").ap())
        return ap

    wupT = scr("wupT", [DM, INTER])
    wgT = scr("wgT", [DM, INTER])
    wdT = scr("wdT", [INTER, DM])
    qT = scr("qTs", [HPC, HD, SEQ])
    kT = scr("kTs", [HPC, HD, SEQ])
    vK = scr("vKs", [HPC, NSTRIP, 128, 512])
    rT = scr("rTs", [DM, TOK])
    a2a_i = nc.dram_tensor("a2a_i", [SEQ, HPC * HD], F32,
                           kind="Internal").ap()
    a2a_o = nc.dram_tensor("a2a_o", [SEQ, HPC * HD], F32,
                           kind="Internal").ap()
    dbg_a2a = (nc.dram_tensor("dbg_a2a", [SEQ, HPC * HD], F32,
                              kind="ExternalOutput").ap() if dbg else None)

    es = contextlib.ExitStack()
    with es:
        AR = 43008
        arena = es.enter_context(nc.sbuf_tensor("arena", [128, AR], F32))
        sb_i128 = es.enter_context(nc.sbuf_tensor("sb_i128", [128, 128], F32))
        sb_t128 = es.enter_context(nc.sbuf_tensor("sb_t128", [128, 128], F32))
        sb_h128a = es.enter_context(
            nc.sbuf_tensor("sb_h128a", [128, 128], F32))
        sb_h128b = es.enter_context(
            nc.sbuf_tensor("sb_h128b", [128, 128], F32))
        ones_c = es.enter_context(nc.sbuf_tensor("ones_c", [128, 1], F32))
        ones_r = es.enter_context(nc.sbuf_tensor("ones_r", [1, 128], F32))
        sb_su = es.enter_context(nc.sbuf_tensor("sb_su", [128, NCH], F32))
        sb_sud = es.enter_context(nc.sbuf_tensor("sb_sud", [128, NIC], F32))
        sb_wu = es.enter_context(nc.sbuf_tensor("sb_wu", [128, NIC], F32))
        sb_wg = es.enter_context(nc.sbuf_tensor("sb_wg", [128, NIC], F32))
        sb_wd = es.enter_context(nc.sbuf_tensor("sb_wd", [128, NCH], F32))
        stat = es.enter_context(nc.sbuf_tensor("stat", [128, 16], F32))
        ebias = es.enter_context(nc.sbuf_tensor("ebias", [128, 1], F32))
        cst = es.enter_context(nc.sbuf_tensor("cst", [128, 3], F32))
        rows = es.enter_context(nc.sbuf_tensor("rows", [1, 4096], F32))

        pb = [es.enter_context(nc.psum_tensor(f"pb{i}", [128, 512], F32))
              for i in range(8)]

        sems = {}
        for sname in ("te", "ve", "sc", "gc", "cc"):
            sems[sname] = es.enter_context(nc.semaphore(sname))
        for lane in range(NSY):
            sems[("sy", lane)] = es.enter_context(nc.semaphore(f"sy{lane}"))
        for lane in range(NGP):
            sems[("gp", lane)] = es.enter_context(nc.semaphore(f"gp{lane}"))

        def A(off, span):
            return arena[:, off:off + span]

        def emit(c: Ctx):
            for dst, src in ((sb_i128, i128), (sb_t128, t128),
                             (sb_h128a, h128a), (sb_h128b, h128b),
                             (sb_su, su_ug),
                             (sb_sud, su_dp), (sb_wu, wsc_u), (sb_wg, wsc_g),
                             (sb_wd, wsc_dp)):
                c.dma("sync", dst[:, :], src[:, :], "sy", w=(CONST,))
            for buf, val in ((ones_c, 1.0), (ones_r, 1.0), (ebias, EBIAS)):
                c.op("vector", lambda E, buf=buf, val=val: E.memset(
                    buf[:, :], val), r=(CONST,), w=(CONST,))
            for j, val in ((0, RD), (1, EPS), (2, -1.0)):
                c.op("vector", lambda E, j=j, val=val: E.memset(
                    cst[:, j:j + 1], val), r=(CONST,), w=(CONST,))

            xb = [A(0, 2048), A(2048, 2048)]
            stg = [A(4096, 8192), A(12288, 8192)]
            nunit = 0
            for (mi, (msrc, mdst, Rr, Cc)) in enumerate(
                    ((wup, wupT, INTER, DM), (wgate, wgT, INTER, DM),
                     (wdown, wdT, DM, INTER))):
                for rb in range(Rr // 512):
                    for cb in range(Cc // 2048):
                        u = nunit
                        nunit += 1
                        sg = stg[u % 2]
                        sgv = sg.rearrange("p (cc r) -> p cc r", r=512)
                        for rt in range(4):
                            r0 = rb * 512 + rt * 128
                            c.dma("sync", xb[rt % 2][:, :],
                                  msrc[r0:r0 + 128,
                                       cb * 2048:(cb + 1) * 2048], "sy",
                                  w=(("xb", rt % 2),))
                            for g in range(4):
                                bi = 6 + g % 2
                                c.grp([(lambda E, bi=bi, k=k, g=g, rt=rt:
                                        E.transpose(
                                            pb[bi][:, k * 128:(k + 1) * 128],
                                            xb[rt % 2][:, (4 * g + k) * 128:
                                                       (4 * g + k + 1) * 128],
                                            sb_i128[:, :]))
                                       for k in range(4)],
                                      r=(("xb", rt % 2), CONST),
                                      w=(("pb", bi),))
                                c.op("scalar", lambda E, bi=bi, g=g, rt=rt,
                                     sgv=sgv: E.copy(
                                         sgv[:, 4 * g:4 * g + 4,
                                             rt * 128:(rt + 1) * 128],
                                         pb[bi][:, :]),
                                     r=(("pb", bi),), w=(("stg", u % 2),))
                        dv = mdst.rearrange("(cc p) r -> p cc r", p=128)[
                            :, cb * 16:(cb + 1) * 16, rb * 512:(rb + 1) * 512]
                        c.dma("gpsimd", dv, sgv, "gp",
                              r=(("stg", u % 2),), w=(("wT", mi, rb, cb),))
            c.barrier()

            wq_sb = A(0, 12288)
            xt = [A(12288, 2048), A(14336, 2048)]
            xnT = A(16384, 8192)
            qkvstg = A(24576, 3072)
            vstg = A(27648, 512)
            sqscr = A(28160, 2048)
            wqv = wq_sb.rearrange("p (cc j) -> p cc j", j=768)
            c.dma("sync", wqv, wqkv.rearrange("(cc p) j -> p cc j", p=128),
                  "sy", w=(("wq",),))
            xnv = xnT.rearrange("p (cc t) -> p cc t", t=512)
            for s in range(NSTRIP):
                for tt in range(4):
                    t0 = s * 512 + tt * 128
                    XT = ("xt", tt % 2)
                    c.dma("sync", xt[tt % 2][:, :], x[t0:t0 + 128, :], "sy",
                          w=(XT,))
                    c.op("vector", lambda E, tt=tt: E.reduce_sum(
                        stat[:, 0:1], xt[tt % 2][:, :], axis=AX),
                        r=(XT,), w=(("st", 0),))
                    c.op("vector", lambda E: E.tensor_scalar(
                        stat[:, 1:2], stat[:, 0:1], cst[:, 0:1], None, MUL),
                        r=(("st", 0), CONST), w=(("st", 1),))
                    c.op("vector", lambda E: E.tensor_mul(
                        stat[:, 2:3], stat[:, 1:2], stat[:, 1:2]),
                        r=(("st", 1),), w=(("st", 2),))
                    c.op("vector", lambda E: E.tensor_scalar(
                        stat[:, 3:4], stat[:, 2:3], cst[:, 2:3], cst[:, 1:2],
                        MUL, ADD), r=(("st", 2), CONST), w=(("st", 3),))
                    c.op("scalar", lambda E, tt=tt: E.activation(
                        sqscr[:, :], xt[tt % 2][:, :], AF.Square,
                        accum_out=stat[:, 4:5]),
                        r=(XT,), w=(("sqscr",), ("st", 4)))
                    c.op("scalar", lambda E: E.activation(
                        stat[:, 5:6], stat[:, 4:5], AF.Sqrt,
                        bias=stat[:, 3:4], scale=RD),
                        r=(("st", 4), ("st", 3)), w=(("st", 5),))
                    c.op("vector", lambda E: E.reciprocal(
                        stat[:, 6:7], stat[:, 5:6]),
                        r=(("st", 5),), w=(("st", 6),))
                    c.op("vector", lambda E, tt=tt: E.tensor_scalar(
                        xt[tt % 2][:, :], xt[tt % 2][:, :],
                        stat[:, 1:2], stat[:, 6:7], SUB, MUL),
                        r=(XT, ("st", 1), ("st", 6)), w=(XT,))
                    for g in range(4):
                        bi = 6 + g % 2
                        c.grp([(lambda E, bi=bi, k=k, g=g, tt=tt: E.transpose(
                            pb[bi][:, k * 128:(k + 1) * 128],
                            xt[tt % 2][:, (4 * g + k) * 128:
                                       (4 * g + k + 1) * 128],
                            sb_i128[:, :])) for k in range(4)],
                            r=(XT, CONST), w=(("pb", bi),))
                        c.op("scalar", lambda E, bi=bi, g=g, tt=tt: E.copy(
                            xnv[:, 4 * g:4 * g + 4, tt * 128:(tt + 1) * 128],
                            pb[bi][:, :]),
                            r=(("pb", bi),), w=(("xnT", g, tt),))
                xnkeys = tuple(("xnT", g, tt) for g in range(4)
                               for tt in range(4))
                for jt in range(6):
                    c.grp([(lambda E, jt=jt, ccx=ccx: E.matmul(
                        pb[jt][:, 0:512],
                        wqv[:, ccx, jt * 128:(jt + 1) * 128],
                        xnv[:, ccx, :], start=(ccx == 0),
                        stop=(ccx == NCH - 1))) for ccx in range(NCH)],
                        r=xnkeys + (("wq",),), w=(("pb", jt),))
                for jt in range(6):
                    fn = (lambda E, jt=jt: E.mul(
                        qkvstg[:, jt * 512:(jt + 1) * 512], pb[jt][:, 0:512],
                        QSC)) if jt < 2 else (lambda E, jt=jt: E.copy(
                            qkvstg[:, jt * 512:(jt + 1) * 512],
                            pb[jt][:, 0:512]))
                    c.op("scalar", fn, r=(("pb", jt),), w=(("qkvstg", jt),))
                for h in range(2):
                    bi = 6 + h
                    c.grp([(lambda E, bi=bi, k=k, h=h: E.transpose(
                        pb[bi][:, k * 128:(k + 1) * 128],
                        qkvstg[:, (4 + h) * 512 + k * 128:
                               (4 + h) * 512 + (k + 1) * 128],
                        sb_i128[:, :])) for k in range(4)],
                        r=(("qkvstg", 4 + h), CONST), w=(("pb", bi),))
                    c.op("vector", lambda E, bi=bi: E.tensor_copy(
                        vstg[:, :], pb[bi][:, :]),
                        r=(("pb", bi),), w=(("vstg",),))
                    c.dma("gpsimd", vK[h, s, :, :], vstg[:, :], "gp",
                          r=(("vstg",),), w=(("vK", h, s),))
                for h in range(2):
                    c.dma("gpsimd", qT[h, :, s * 512:(s + 1) * 512],
                          qkvstg[:, h * 512:(h + 1) * 512], "gp",
                          r=(("qkvstg", h),), w=(("qT", h, s),))
                    c.dma("gpsimd", kT[h, :, s * 512:(s + 1) * 512],
                          qkvstg[:, (2 + h) * 512:(3 + h) * 512], "gp",
                          r=(("qkvstg", 2 + h),), w=(("kT", h, s),))
            c.barrier()

            ktb = A(0, 8192)
            vaug = A(8192, 8256)
            qb = [A(16448, 512), A(16960, 512)]
            pbf = [A(17472, 512), A(17984, 512)]
            ostg = [A(18496, 128), A(18624, 128)]
            vaugv = vaug.rearrange("p (k e) -> p k e", e=129)
            for h in range(2):
                kkeys = tuple(("kT", h, s2) for s2 in range(NSTRIP))
                vkeys = tuple(("vK", h, s2) for s2 in range(NSTRIP))
                c.dma("sync", ktb[:, :], kT[h, :, :], "sy",
                      r=kkeys, w=(("ktb",),))
                for s2 in range(NSTRIP):
                    c.dma("sync", vaugv[:, s2 * 4:(s2 + 1) * 4, 0:128],
                          vK[h, s2].rearrange("p (kb d) -> p kb d", d=128),
                          "sy", r=(("vK", h, s2),), w=(("vaug",),))
                c.op("vector", lambda E: E.memset(vaugv[:, :, 128:129], 1.0),
                     r=(), w=(("vaug1",),))
                for s in range(NSTRIP):
                    c.dma("sync", qb[s % 2][:, :],
                          qT[h, :, s * 512:(s + 1) * 512], "sy",
                          r=(("qT", h, s),), w=(("qb", s % 2),))
                    nkc = 4 * s + 4

                    def emit_s(kc, s=s):
                        sbk = kc % 2
                        diag = kc - 4 * s
                        c.grp([lambda E, sbk=sbk, kc=kc, s=s: E.matmul(
                            pb[sbk][:, 0:512],
                            ktb[:, kc * 128:(kc + 1) * 128],
                            qb[s % 2][:, :], start=True, stop=True)],
                            r=(("ktb",), ("qb", s % 2)), w=(("pb", sbk),))
                        o0 = 128 * diag if diag >= 0 else 0
                        c.op("scalar", lambda E, sbk=sbk, kc=kc, o0=o0:
                             E.activation(pbf[kc % 2][:, o0:512],
                                          pb[sbk][:, o0:512],
                                          AF.Exp, bias=ebias[:, 0:1]),
                             r=(("pb", sbk), CONST), w=(("pbf", kc % 2),))
                        if diag >= 0:
                            c.op("vector", lambda E, kc=kc, o0=o0:
                                 E.tensor_mul(
                                     pbf[kc % 2][:, o0:o0 + 128],
                                     pbf[kc % 2][:, o0:o0 + 128],
                                     sb_t128[:, :]),
                                 r=(("pbf", kc % 2), CONST),
                                 w=(("pbf", kc % 2),))

                    def emit_pv(kc, s=s):
                        diag = kc - 4 * s
                        t_lo = max(diag, 0)
                        for t in range(t_lo, 4):
                            stop = (kc == 4 * s + t)
                            c.grp([lambda E, t=t, kc=kc, stop=stop: E.matmul(
                                pb[2 + t][:, 0:129],
                                pbf[kc % 2][:, t * 128:(t + 1) * 128],
                                vaugv[:, kc, :],
                                start=(kc == 0), stop=stop)],
                                r=(("pbf", kc % 2), ("vaug",), ("vaug1",)),
                                w=(("pb", 2 + t),) if kc == 0 else
                                (("pbacc", 2 + t),))

                    emit_s(0)
                    for kc in range(1, nkc):
                        emit_s(kc)
                        emit_pv(kc - 1)
                    emit_pv(nkc - 1)
                    for t in range(4):
                        c.op("vector", lambda E, t=t: E.reciprocal(
                            stat[:, 8:9], pb[2 + t][:, 128:129]),
                            r=(("pb", 2 + t), ("pbacc", 2 + t)),
                            w=(("st", 8),))
                        c.op("vector", lambda E, t=t: E.tensor_scalar(
                            ostg[t % 2][:, :], pb[2 + t][:, 0:128],
                            stat[:, 8:9], None, MUL),
                            r=(("pb", 2 + t), ("pbacc", 2 + t), ("st", 8)),
                            w=(("ostg", t % 2),))
                        c.dma("gpsimd",
                              a2a_i[s * 512 + t * 128:s * 512 + (t + 1) * 128,
                                    h * 128:(h + 1) * 128],
                              ostg[t % 2][:, :], "gp",
                              r=(("ostg", t % 2),), w=(("a2a_i", h, s, t),))

            if c.eng == "gpsimd":
                c.wq("gp")
            with allow_ici_all_to_all():
                if c.eng == "gpsimd":
                    c.E.collective_compute(
                        "AllToAll", mybir.AluOpType.bypass,
                        replica_groups=[list(range(NCORE))],
                        ins=[a2a_i.opt()], outs=[a2a_o.opt()],
                    ).then_inc(c.sems["cc"], 1)
            c.cnt["cc"] += 1
            c._fin(("cc", c.cnt["cc"], "gpsimd"), (), (("a2a_o",),))
            if dbg:
                c.dma("gpsimd", dbg_a2a[:, :], a2a_o[:, :], "gp",
                      r=(("a2a_o",),), w=())
            c.barrier()

            attnT = A(0, 16384)
            yT = A(16384, 16384)
            wob = [A(32768, 2048), A(34816, 2048)]
            abuf = [A(36864, 128), A(36992, 128)]
            sq2 = A(37120, 1024)
            xt2 = A(38144, 2048)
            attv = attnT.rearrange("p (cc t) -> p cc t", t=1024)
            for ch in range(NCH):
                cb, col0 = ch // 2, (ch % 2) * 128
                for g in range(2):
                    bi = g % 2
                    for k in range(4):
                        tt = g * 4 + k
                        c.dma("sync", abuf[tt % 2][:, :],
                              a2a_o[cb * 1024 + tt * 128:
                                    cb * 1024 + (tt + 1) * 128,
                                    col0:col0 + 128], "sy",
                              r=(("a2a_o",),), w=(("abuf", tt % 2),))
                        c.grp([lambda E, bi=bi, k=k, tt=tt: E.transpose(
                            pb[bi][:, k * 128:(k + 1) * 128],
                            abuf[tt % 2][:, :], sb_i128[:, :])],
                            r=(("abuf", tt % 2), CONST), w=(("pb", bi),))
                    c.op("scalar", lambda E, bi=bi, ch=ch, g=g: E.copy(
                        attv[:, ch, g * 512:(g + 1) * 512], pb[bi][:, :]),
                        r=(("pb", bi),), w=(("attnT", ch),))

            woutv = wout.rearrange("(cc p) j -> p cc j", p=128)
            akeys = tuple(("attnT", ch) for ch in range(NCH))
            for jt in range(NCH):
                c.dma("sync",
                      wob[jt % 2].rearrange("p (cc j) -> p cc j", j=128),
                      woutv[:, :, jt * 128:(jt + 1) * 128], "sy",
                      w=(("wob", jt % 2),))
                wv = wob[jt % 2].rearrange("p (cc j) -> p cc j", j=128)
                for half in range(2):
                    bi = 2 * (jt % 2) + half
                    c.grp([(lambda E, bi=bi, wv=wv, ccx=ccx, half=half:
                            E.matmul(
                                pb[bi][:, 0:512], wv[:, ccx, :],
                                attv[:, ccx, half * 512:(half + 1) * 512],
                                start=(ccx == 0), stop=(ccx == NCH - 1)))
                           for ccx in range(NCH)],
                        r=akeys + (("wob", jt % 2),), w=(("pb", bi),))
                    c.op("scalar", lambda E, bi=bi, jt=jt, half=half: E.copy(
                        yT[:, jt * 1024 + half * 512:
                           jt * 1024 + (half + 1) * 512], pb[bi][:, :]),
                        r=(("pb", bi),), w=(("yT", jt),))

            def fm_ln(zread_fn, nch, width, sqbuf, sqkey, rowoff, banks,
                      apply_fn):
                b0, b1, b2, b3 = banks
                for ccx in range(nch):
                    ap, rk = zread_fn(ccx)
                    c.op("scalar", lambda E, ap=ap: E.activation(
                        sqbuf, ap, AF.Square), r=rk, w=(sqkey,))
                    c.grp([lambda E, ap=ap, ccx=ccx: E.matmul(
                        pb[b0][0:1, 0:width], ones_c[:, :], ap,
                        start=(ccx == 0), stop=(ccx == nch - 1))],
                        r=rk + (CONST,),
                        w=((("pb", b0),) if ccx == 0 else (("pbacc", b0),)))
                    c.grp([lambda E, ccx=ccx: E.matmul(
                        pb[b1][0:1, 0:width], ones_c[:, :], sqbuf,
                        start=(ccx == 0), stop=(ccx == nch - 1))],
                        r=(sqkey, CONST),
                        w=((("pb", b1),) if ccx == 0 else (("pbacc", b1),)))
                ro = rows[0:1, rowoff:rowoff + width]
                ro2 = rows[0:1, rowoff + width:rowoff + 2 * width]
                RO, RO2 = ("ro", rowoff), ("ro2", rowoff)
                PB0 = (("pb", b0), ("pbacc", b0))
                PB1 = (("pb", b1), ("pbacc", b1))
                c.op("vector", lambda E: E.tensor_scalar(
                    ro, pb[b0][0:1, 0:width], cst[0:1, 0:1], None, MUL),
                    r=PB0 + (CONST,), w=(RO,))
                c.op("vector", lambda E: E.tensor_scalar(
                    ro2, pb[b1][0:1, 0:width], cst[0:1, 0:1], None, MUL),
                    r=PB1 + (CONST,), w=(RO2,))
                c.op("vector", lambda E: E.tensor_mul(
                    pb[b0][0:1, 0:width], ro, ro),
                    r=(RO,) + PB0, w=(("pb", b0),))
                c.op("vector", lambda E: E.tensor_sub(
                    ro2, ro2, pb[b0][0:1, 0:width]),
                    r=(RO2, ("pb", b0)), w=(RO2,))
                c.op("vector", lambda E: E.tensor_scalar(
                    ro2, ro2, cst[0:1, 1:2], None, ADD),
                    r=(RO2, CONST), w=(RO2,))
                c.op("scalar", lambda E: E.activation(ro2, ro2, AF.Sqrt),
                     r=(RO2,), w=(RO2,))
                c.op("vector", lambda E: E.reciprocal(ro2, ro2),
                     r=(RO2,), w=(RO2,))
                c.op("vector", lambda E: E.tensor_mul(ro, ro, ro2),
                     r=(RO, RO2), w=(RO,))
                c.grp([lambda E: E.matmul(
                    pb[b2][:, 0:width], ones_r[0:1, :], ro2,
                    start=True, stop=True)], r=(RO2, CONST), w=(("pb", b2),))
                c.grp([lambda E: E.matmul(
                    pb[b3][:, 0:width], ones_r[0:1, :], ro,
                    start=True, stop=True)], r=(RO, CONST), w=(("pb", b3),))
                for ccx in range(nch):
                    apply_fn(ccx, pb[b2][:, 0:width], pb[b3][:, 0:width],
                             (("pb", b2), ("pb", b3)))

            for half in range(2):
                hoff = half * 512

                def yslice(ccx, hoff=hoff):
                    return yT[:, ccx * 1024 + hoff:ccx * 1024 + hoff + 512]

                def ap5(ccx, s_ap, ms_ap, bk):
                    c.op("vector", lambda E, ccx=ccx, s_ap=s_ap: E.tensor_mul(
                        yslice(ccx), yslice(ccx), s_ap),
                        r=(("yT", ccx),) + bk, w=(("yT", ccx),))
                    c.op("vector", lambda E, ccx=ccx, ms_ap=ms_ap:
                         E.tensor_sub(yslice(ccx), yslice(ccx), ms_ap),
                         r=(("yT", ccx),) + bk, w=(("yT", ccx),))
                fm_ln(lambda ccx: (yslice(ccx), (("yT", ccx),)),
                      NCH, 512, sq2[:, 0:512], ("sq2",),
                      1024 * half, (4, 5, 6, 7), ap5)
            yv1 = yT.rearrange("p (cc t) -> p cc t", t=1024)
            for tt in range(8):
                c.dma("sync", xt2[:, :], xsl[tt * 128:(tt + 1) * 128, :],
                      "sy", w=(("xt2",),))
                for g in range(4):
                    bi = g % 2
                    c.grp([(lambda E, bi=bi, k=k, g=g: E.transpose(
                        pb[bi][:, k * 128:(k + 1) * 128],
                        xt2[:, (4 * g + k) * 128:(4 * g + k + 1) * 128],
                        sb_i128[:, :])) for k in range(4)],
                        r=(("xt2",), CONST), w=(("pb", bi),))
                    ykeys = tuple(("yT", 4 * g + k) for k in range(4))
                    c.op("vector", lambda E, bi=bi, g=g, tt=tt:
                         E.tensor_tensor(
                             yv1[:, 4 * g:4 * g + 4, tt * 128:(tt + 1) * 128],
                             yv1[:, 4 * g:4 * g + 4, tt * 128:(tt + 1) * 128],
                             pb[bi][:, :], ADD),
                         r=ykeys + (("pb", bi),), w=ykeys)
            c.dma("gpsimd", rT.rearrange("(cc p) t -> p cc t", p=128),
                  yT.rearrange("p (cc t) -> p cc t", t=1024), "gp",
                  r=tuple(("yT", j) for j in range(NCH)), w=(("rT",),))
            c.barrier()

            rsl = A(0, 4096)
            zb = A(4096, 4096)
            h2 = A(8192, 16384)
            wb = [A(24576, 2048), A(26624, 2048),
                  A(28672, 2048), A(30720, 2048)]
            hh = [A(32768, 256), A(42496, 256)]
            sq6 = [A(33024, 256), A(42752, 256)]
            tmp6 = A(33280, 256)
            ostg2 = A(33536, 2048)
            z1 = A(35584, 4096)
            rslv = rsl.rearrange("p (cc t) -> p cc t", t=FT)
            zv = zb.rearrange("p (cc t) -> p cc t", t=FT)
            z1v = z1.rearrange("p (cc t) -> p cc t", t=FT)
            y2v = zb.rearrange("p (cc t) -> p cc t", t=FT)
            h2v = h2.rearrange("p (ic t) -> p ic t", t=FT)
            wupTv = wupT.rearrange("(cc p) i -> p cc i", p=128)
            wgTv = wgT.rearrange("(cc p) i -> p cc i", p=128)
            wdTv = wdT.rearrange("(ic p) j -> p ic j", p=128)
            rTv = rT.rearrange("(cc p) t -> p cc t", p=128)
            zbkeys = tuple(("zb", ccx) for ccx in range(NCH))

            for sub in range(NSUB):
                toff = sub * FT
                c.dma("sync", rslv, rTv[:, :, toff:toff + FT], "sy",
                      r=(("rT",),), w=(("rsl",),))

                def apz(ccx, s_ap, ms_ap, bk):
                    c.op("vector", lambda E, ccx=ccx, s_ap=s_ap:
                         E.tensor_tensor(zv[:, ccx, :], rslv[:, ccx, :],
                                         s_ap, MUL),
                         r=(("rsl",),) + bk, w=(("zb", ccx),))
                    c.op("vector", lambda E, ccx=ccx, ms_ap=ms_ap:
                         E.tensor_sub(zv[:, ccx, :], zv[:, ccx, :], ms_ap),
                         r=(("zb", ccx),) + bk, w=(("zb", ccx),))
                    c.op("vector", lambda E, ccx=ccx: E.tensor_scalar(
                        zv[:, ccx, :], zv[:, ccx, :], sb_su[:, ccx:ccx + 1],
                        None, MUL), r=(("zb", ccx), CONST), w=(("zb", ccx),))
                fm_ln(lambda ccx: (rslv[:, ccx, :], (("rsl",),)),
                      NCH, FT, sq6[0][:, :], ("sq6", 0), 2048,
                      (0, 1, 2, 3), apz)
                for ccx in range(NCH):
                    bi = 4 + ccx % 2
                    c.grp([lambda E, bi=bi, ccx=ccx: E.matmul(
                        pb[bi][:, 0:FT], sb_h128a[:, :], zv[:, ccx, :],
                        start=True, stop=True)],
                        r=(("zb", ccx), CONST), w=(("pb", bi),))
                    c.op("scalar", lambda E, bi=bi, ccx=ccx: E.copy(
                        z1v[:, ccx, :], pb[bi][:, 0:FT]),
                        r=(("pb", bi),), w=(("z1", ccx),))
                for st in range(4):
                    bit = 1 << st
                    for i in range(NCH):
                        if i & bit:
                            continue
                        j = i | bit
                        c.op("vector", lambda E, i=i, j=j: E.tensor_sub(
                            tmp6[:, :], z1v[:, i, :], z1v[:, j, :]),
                            r=(("z1", i), ("z1", j)), w=(("tmp6",),))
                        c.op("vector", lambda E, i=i, j=j: E.tensor_add(
                            z1v[:, i, :], z1v[:, i, :], z1v[:, j, :]),
                            r=(("z1", i), ("z1", j)), w=(("z1", i),))
                        c.op("vector", lambda E, j=j: E.tensor_copy(
                            z1v[:, j, :], tmp6[:, :]),
                            r=(("tmp6",),), w=(("z1", j),))
                z1keys = tuple(("z1", ccx) for ccx in range(NCH))
                def emit_h2(ic):
                    bh = 4 + ic % 2
                    c.grp([lambda E, bh=bh, ic=ic: E.matmul(
                        pb[bh][:, 0:FT], sb_h128b[:, :], hh[ic % 2][:, :],
                        start=True, stop=True)],
                        r=(("hh", ic % 2), CONST), w=(("pb", bh),))
                    c.op("scalar", lambda E, bh=bh, ic=ic: E.copy(
                        h2v[:, ic, :], pb[bh][:, 0:FT]),
                        r=(("pb", bh),), w=(("h2", ic),))

                for ic in range(NIC):
                    c.dma("sync",
                          wb[ic % 2].rearrange("p (cc j) -> p cc j", j=128),
                          wupTv[:, :, ic * 128:(ic + 1) * 128], "sy",
                          r=(("wT", 0, ic // 4, 0),), w=(("wb", ic % 2),))
                    c.dma("sync",
                          wb[2 + ic % 2].rearrange("p (cc j) -> p cc j",
                                                   j=128),
                          wgTv[:, :, ic * 128:(ic + 1) * 128], "sy",
                          r=(("wT", 1, ic // 4, 0),), w=(("wb", 2 + ic % 2),))
                    buv = wb[ic % 2].rearrange("p (cc j) -> p cc j", j=128)
                    bgv = wb[2 + ic % 2].rearrange("p (cc j) -> p cc j",
                                                   j=128)
                    bu, bg = ic % 2, 2 + ic % 2
                    c.grp([(lambda E, bu=bu, buv=buv, ccx=ccx: E.matmul(
                        pb[bu][:, 0:FT], buv[:, ccx, :], z1v[:, ccx, :],
                        start=(ccx == 0), stop=(ccx == NCH - 1)))
                        for ccx in range(NCH)],
                        r=z1keys + (("wb", bu),), w=(("pb", bu),))
                    c.grp([(lambda E, bg=bg, bgv=bgv, ccx=ccx: E.matmul(
                        pb[bg][:, 0:FT], bgv[:, ccx, :], z1v[:, ccx, :],
                        start=(ccx == 0), stop=(ccx == NCH - 1)))
                        for ccx in range(NCH)],
                        r=z1keys + (("wb", bg),), w=(("pb", bg),))
                    c.op("scalar", lambda E, bg=bg, ic=ic: E.activation(
                        sq6[ic % 2][:, :], pb[bg][:, 0:FT], AF.Silu,
                        scale=sb_wg[:, ic:ic + 1]),
                        r=(("pb", bg), CONST), w=(("sq6", ic % 2),))
                    c.op("vector", lambda E, bu=bu, ic=ic: E.tensor_scalar(
                        hh[ic % 2][:, :], pb[bu][:, 0:FT],
                        sb_wu[:, ic:ic + 1], sb_sud[:, ic:ic + 1], MUL, MUL),
                        r=(("pb", bu), CONST), w=(("hh", ic % 2),))
                    c.op("vector", lambda E, ic=ic: E.tensor_mul(
                        hh[ic % 2][:, :], hh[ic % 2][:, :],
                        sq6[ic % 2][:, :]),
                        r=(("hh", ic % 2), ("sq6", ic % 2)),
                        w=(("hh", ic % 2),))
                    if ic > 0:
                        emit_h2(ic - 1)
                emit_h2(NIC - 1)
                for st in range(6):
                    bit = 1 << st
                    for i in range(NIC):
                        if i & bit:
                            continue
                        j = i | bit
                        c.op("vector", lambda E, i=i, j=j: E.tensor_sub(
                            tmp6[:, :], h2v[:, i, :], h2v[:, j, :]),
                            r=(("h2", i), ("h2", j)), w=(("tmp6",),))
                        c.op("vector", lambda E, i=i, j=j: E.tensor_add(
                            h2v[:, i, :], h2v[:, i, :], h2v[:, j, :]),
                            r=(("h2", i), ("h2", j)), w=(("h2", i),))
                        c.op("vector", lambda E, j=j: E.tensor_copy(
                            h2v[:, j, :], tmp6[:, :]),
                            r=(("tmp6",),), w=(("h2", j),))
                h2keys = tuple(("h2", ic) for ic in range(NIC))
                for jt in range(NCH):
                    bd = 6 + jt % 2
                    for q in range(4):
                        c.dma("sync",
                              wb[q % 2].rearrange("p (ic j) -> p ic j",
                                                  j=128),
                              wdTv[:, q * 16:(q + 1) * 16,
                                   jt * 128:(jt + 1) * 128], "sy",
                              r=(("wT", 2, jt // 4, q),),
                              w=(("wb", q % 2),))
                        wdv = wb[q % 2].rearrange("p (ic j) -> p ic j", j=128)
                        c.grp([(lambda E, bd=bd, wdv=wdv, icq=icq, q=q:
                                E.matmul(
                                    pb[bd][:, 0:FT], wdv[:, icq, :],
                                    h2v[:, q * 16 + icq, :],
                                    start=(q * 16 + icq == 0),
                                    stop=(q * 16 + icq == NIC - 1)))
                               for icq in range(16)],
                            r=h2keys + (("wb", q % 2),),
                            w=((("pb", bd),) if q == 0 else
                               (("pbacc", bd),)))
                    c.op("vector", lambda E, jt=jt, bd=bd: E.tensor_scalar(
                        y2v[:, jt, :], pb[bd][:, 0:FT],
                        sb_wd[:, jt:jt + 1], None, MUL),
                        r=(("pb", bd), ("pbacc", bd), CONST),
                        w=(("zb", jt),))

                def apo(ccx, s_ap, ms_ap, bk):
                    c.op("vector", lambda E, ccx=ccx, s_ap=s_ap: E.tensor_mul(
                        y2v[:, ccx, :], y2v[:, ccx, :], s_ap),
                        r=(("zb", ccx),) + bk, w=(("zb", ccx),))
                    c.op("vector", lambda E, ccx=ccx, ms_ap=ms_ap:
                         E.tensor_sub(y2v[:, ccx, :], y2v[:, ccx, :], ms_ap),
                         r=(("zb", ccx),) + bk, w=(("zb", ccx),))
                    c.op("vector", lambda E, ccx=ccx: E.tensor_tensor(
                        y2v[:, ccx, :], y2v[:, ccx, :], rslv[:, ccx, :],
                        ADD), r=(("zb", ccx), ("rsl",)), w=(("zb", ccx),))
                fm_ln(lambda ccx: (y2v[:, ccx, :], (("zb", ccx),)),
                      NCH, FT, sq6[0][:, :], ("sq6", 0), 2048,
                      (0, 1, 2, 3), apo)
                for tt2 in range(2):
                    for g in range(4):
                        bi = 4 + g % 2
                        zkeys = tuple(("zb", 4 * g + k) for k in range(4))
                        c.grp([(lambda E, bi=bi, k=k, g=g, tt2=tt2:
                                E.transpose(
                                    pb[bi][:, k * 128:(k + 1) * 128],
                                    y2v[:, 4 * g + k,
                                        tt2 * 128:(tt2 + 1) * 128],
                                    sb_i128[:, :])) for k in range(4)],
                            r=zkeys + (CONST,), w=(("pb", bi),))
                        c.op("scalar", lambda E, bi=bi, g=g: E.copy(
                            ostg2[:, g * 512:(g + 1) * 512], pb[bi][:, :]),
                            r=(("pb", bi),), w=(("ostg2", g),))
                    c.dma("gpsimd",
                          out[toff + tt2 * 128:toff + (tt2 + 1) * 128, :],
                          ostg2[:, :], "gp",
                          r=tuple(("ostg2", g) for g in range(4)), w=())
            if dbg == 2:
                c.barrier()
                for name, (src, dst) in dbg_out.items():
                    nd = len(src.tensor.shape)
                    sl = (slice(None),) * nd
                    c.dma("gpsimd", dst[sl], src[sl], "gp")
            if c.eng == "gpsimd":
                c.wq("gp")

        with nc.Block() as block:
            @block.sync
            def _(e):
                emit(Ctx("sync", e, sems))

            @block.tensor
            def _(e):
                emit(Ctx("tensor", e, sems))

            @block.vector
            def _(e):
                emit(Ctx("vector", e, sems))

            @block.scalar
            def _(e):
                emit(Ctx("scalar", e, sems))

            @block.gpsimd
            def _(e):
                emit(Ctx("gpsimd", e, sems))

    return nc


def kernel(L0_attn_x, L0_attn_w_qkv, L0_attn_w_out, L0_ffn_SU_ug, L0_ffn_SU_dp,
           L0_ffn_Wscale_ug, L0_ffn_Wscale_dp, L0_ffn_W_up, L0_ffn_W_gate,
           L0_ffn_W_down):
    cont = np.ascontiguousarray
    x = cont(np.asarray(L0_attn_x, dtype=np.float32).reshape(SEQ, DM))
    wqkv = np.asarray(L0_attn_w_qkv, dtype=np.float32)
    base = {
        "x": x,
        "wout": cont(np.asarray(L0_attn_w_out, dtype=np.float32)),
        "wup": cont(np.asarray(L0_ffn_W_up, dtype=np.float32)),
        "wgate": cont(np.asarray(L0_ffn_W_gate, dtype=np.float32)),
        "wdown": cont(np.asarray(L0_ffn_W_down, dtype=np.float32)),
        "su_ug": cont(np.asarray(L0_ffn_SU_ug, dtype=np.float32)
                      .reshape(NCH, 128).T),
        "su_dp": cont(np.asarray(L0_ffn_SU_dp, dtype=np.float32)
                      .reshape(NIC, 128).T),
        "wsc_u": cont(np.asarray(L0_ffn_Wscale_ug, dtype=np.float32)[:INTER]
                      .reshape(NIC, 128).T),
        "wsc_g": cont(np.asarray(L0_ffn_Wscale_ug, dtype=np.float32)[INTER:]
                      .reshape(NIC, 128).T),
        "wsc_dp": cont(np.asarray(L0_ffn_Wscale_dp, dtype=np.float32)
                       .reshape(NCH, 128).T),
        "i128": np.eye(128, dtype=np.float32),
        "t128": (np.arange(128)[None, :] >=
                 np.arange(128)[:, None]).astype(np.float32),
        "h128a": hadamard(128) / np.float32(np.sqrt(2048.0)),
        "h128b": hadamard(128) / np.float32(np.sqrt(8192.0)),
    }
    nc = build_nc()
    in_maps = []
    for cidx in range(NCORE):
        h0, h1 = 2 * cidx, 2 * cidx + 1
        wq_s = cont(np.concatenate(
            [wqkv[:, h0 * HD:(h0 + 1) * HD],
             wqkv[:, h1 * HD:(h1 + 1) * HD],
             wqkv[:, DM + h0 * HD:DM + (h0 + 1) * HD],
             wqkv[:, DM + h1 * HD:DM + (h1 + 1) * HD],
             wqkv[:, 2 * DM + h0 * HD:2 * DM + (h0 + 1) * HD],
             wqkv[:, 2 * DM + h1 * HD:2 * DM + (h1 + 1) * HD]], axis=1))
        m = dict(base)
        m["wqkv"] = wq_s
        m["xsl"] = cont(x[cidx * TOK:(cidx + 1) * TOK, :])
        in_maps.append(m)
    res = run_bass_kernel_spmd(nc, in_maps, core_ids=list(range(NCORE)))
    outp = np.concatenate([res.results[cidx]["out"] for cidx in range(NCORE)],
                          axis=0)
    return outp.reshape(1, SEQ, DM)


# version 06
# speedup vs baseline: 1.1244x; 1.0066x over previous
import contextlib
import numpy as np

import concourse.bass as bass
import concourse.mybir as mybir
from concourse.bass_utils import run_bass_kernel_spmd
from concourse.aot_env import allow_ici_all_to_all

F32 = mybir.dt.float32
AX = mybir.AxisListType.X
ADD = mybir.AluOpType.add
SUB = mybir.AluOpType.subtract
MUL = mybir.AluOpType.mult
AF = mybir.ActivationFunctionType

SEQ, DM, NH, HD, NCORE = 8192, 2048, 16, 128, 8
HPC = NH // NCORE
TOK = SEQ // NCORE
INTER = 8192
QS = 512
NSTRIP = SEQ // QS
FT = 256
NSUB = TOK // FT
NCH = DM // 128
NIC = INTER // 128
EPS = 1e-5
EBIAS = -20.0
QSC = float(1.0 / np.sqrt(128.0))
RD = float(1.0 / DM)

OWN = {"tensor": "te", "vector": "ve", "scalar": "sc", "gpsimd": "gc"}
NSY = 8
NGP = 4
CONST = ("const",)


class Ctx:

    def __init__(self, eng, E, sems):
        self.eng = eng
        self.E = E
        self.sems = sems
        self.cnt = {k: 0 for k in sems}
        self._lastw = {}
        self._rr = {"sy": 0, "gp": 0}
        self.last_w = {}
        self.readers = {}

    def w(self, sem, val):
        if val is None or val <= 0:
            return
        if self._lastw.get(sem, -1) >= val:
            return
        self._lastw[sem] = val
        if self.E is not None:
            self.E.wait_ge(self.sems[sem], val)

    def _dep(self, eng, reads, writes):
        emitting = (self.eng == eng)
        for k in reads:
            t = self.last_w.get(k)
            if t is not None and not (eng == "tensor" and t[2] == "tensor"):
                if emitting:
                    self.w(t[0], t[1])
        for k in writes:
            t = self.last_w.get(k)
            if t is not None and not (eng == "tensor" and t[2] == "tensor"):
                if emitting:
                    self.w(t[0], t[1])
            for rt in self.readers.get(k, ()):
                if eng == "tensor" and rt[2] == "tensor":
                    continue
                if emitting:
                    self.w(rt[0], rt[1])

    def _fin(self, tok, reads, writes):
        for k in writes:
            self.last_w[k] = tok
            self.readers[k] = []
        for k in reads:
            self.readers.setdefault(k, []).append(tok)

    def op(self, eng, fn, r=(), w=()):
        self._dep(eng, r, w)
        sem = OWN[eng]
        if self.eng == eng:
            fn(self.E).then_inc(self.sems[sem], 1)
        self.cnt[sem] += 1
        tok = (sem, self.cnt[sem], eng)
        self._fin(tok, r, w)
        return tok

    def grp(self, fns, r=(), w=()):
        self._dep("tensor", r, w)
        n = len(fns)
        if self.eng == "tensor":
            for i, fn in enumerate(fns):
                ins = fn(self.E)
                if i == n - 1:
                    ins.then_inc(self.sems["te"], 1)
        self.cnt["te"] += 1
        tok = ("te", self.cnt["te"], "tensor")
        self._fin(tok, r, w)
        return tok

    def dma(self, eng, out, in_, q, r=(), w=()):
        self._dep(eng, r, w)
        n = NSY if q == "sy" else NGP
        lane = self._rr[q]
        self._rr[q] = (lane + 1) % n
        sem = (q, lane)
        if self.eng == eng:
            self.w(sem, self.cnt[sem])
            self.E.dma_start(out=out, in_=in_).then_inc(self.sems[sem], 16)
        self.cnt[sem] += 16
        tok = (sem, self.cnt[sem], eng)
        self._fin(tok, r, w)
        return tok

    def wq(self, q):
        n = NSY if q == "sy" else NGP
        for lane in range(n):
            self.w((q, lane), self.cnt[(q, lane)])

    def barrier(self):
        for sem in ("te", "ve", "sc", "gc", "cc"):
            self.w(sem, self.cnt[sem])
        self.wq("sy")
        self.wq("gp")


def hadamard(n):
    h = np.array([[1.0]], dtype=np.float32)
    while h.shape[0] < n:
        h = np.block([[h, h], [h, -h]])
    return h.astype(np.float32)


def build_nc(dbg=False):
    nc = bass.Bass("TRN2", target_bir_lowering=False)

    def din(name, shape):
        return nc.dram_tensor(name, shape, F32, kind="ExternalInput").ap()

    x = din("x", [SEQ, DM])
    xsl = din("xsl", [TOK, DM])
    wqkv = din("wqkv", [DM, 6 * HD])
    wout = din("wout", [DM, DM])
    wup = din("wup", [INTER, DM])
    wgate = din("wgate", [INTER, DM])
    wdown = din("wdown", [DM, INTER])
    su_ug = din("su_ug", [128, NCH])
    su_dp = din("su_dp", [128, NIC])
    wsc_u = din("wsc_u", [128, NIC])
    wsc_g = din("wsc_g", [128, NIC])
    wsc_dp = din("wsc_dp", [128, NCH])
    i128 = din("i128", [128, 128])
    t128 = din("t128", [128, 128])
    h128a = din("h128a", [128, 128])
    h128b = din("h128b", [128, 128])

    out = nc.dram_tensor("out", [TOK, DM], F32, kind="ExternalOutput").ap()

    dbg_out = {}

    def scr(name, shape):
        kind = "ExternalOutput" if dbg == 1 else "Internal"
        ap = nc.dram_tensor(name, shape, F32, kind=kind).ap()
        if dbg == 2:
            dbg_out[name] = (ap, nc.dram_tensor(
                "dbg_" + name, shape, F32, kind="ExternalOutput").ap())
        return ap

    wupT = scr("wupT", [DM, INTER])
    wgT = scr("wgT", [DM, INTER])
    wdT = scr("wdT", [INTER, DM])
    qT = scr("qTs", [HPC, HD, SEQ])
    kT = scr("kTs", [HPC, HD, SEQ])
    vK = scr("vKs", [HPC, NSTRIP, 128, 512])
    rT = scr("rTs", [DM, TOK])
    a2a_i = nc.dram_tensor("a2a_i", [SEQ, HPC * HD], F32,
                           kind="Internal").ap()
    a2a_o = nc.dram_tensor("a2a_o", [SEQ, HPC * HD], F32,
                           kind="Internal").ap()
    dbg_a2a = (nc.dram_tensor("dbg_a2a", [SEQ, HPC * HD], F32,
                              kind="ExternalOutput").ap() if dbg else None)

    es = contextlib.ExitStack()
    with es:
        AR = 43008
        arena = es.enter_context(nc.sbuf_tensor("arena", [128, AR], F32))
        sb_i128 = es.enter_context(nc.sbuf_tensor("sb_i128", [128, 128], F32))
        sb_t128 = es.enter_context(nc.sbuf_tensor("sb_t128", [128, 128], F32))
        sb_h128a = es.enter_context(
            nc.sbuf_tensor("sb_h128a", [128, 128], F32))
        sb_h128b = es.enter_context(
            nc.sbuf_tensor("sb_h128b", [128, 128], F32))
        ones_c = es.enter_context(nc.sbuf_tensor("ones_c", [128, 1], F32))
        ones_r = es.enter_context(nc.sbuf_tensor("ones_r", [1, 128], F32))
        sb_su = es.enter_context(nc.sbuf_tensor("sb_su", [128, NCH], F32))
        sb_sud = es.enter_context(nc.sbuf_tensor("sb_sud", [128, NIC], F32))
        sb_wu = es.enter_context(nc.sbuf_tensor("sb_wu", [128, NIC], F32))
        sb_wg = es.enter_context(nc.sbuf_tensor("sb_wg", [128, NIC], F32))
        sb_wd = es.enter_context(nc.sbuf_tensor("sb_wd", [128, NCH], F32))
        stat = es.enter_context(nc.sbuf_tensor("stat", [128, 16], F32))
        ebias = es.enter_context(nc.sbuf_tensor("ebias", [128, 1], F32))
        cst = es.enter_context(nc.sbuf_tensor("cst", [128, 3], F32))
        rows = es.enter_context(nc.sbuf_tensor("rows", [1, 4096], F32))

        pb = [es.enter_context(nc.psum_tensor(f"pb{i}", [128, 512], F32))
              for i in range(8)]

        sems = {}
        for sname in ("te", "ve", "sc", "gc", "cc"):
            sems[sname] = es.enter_context(nc.semaphore(sname))
        for lane in range(NSY):
            sems[("sy", lane)] = es.enter_context(nc.semaphore(f"sy{lane}"))
        for lane in range(NGP):
            sems[("gp", lane)] = es.enter_context(nc.semaphore(f"gp{lane}"))

        def A(off, span):
            return arena[:, off:off + span]

        def emit(c: Ctx):
            for dst, src in ((sb_i128, i128), (sb_t128, t128),
                             (sb_h128a, h128a), (sb_h128b, h128b),
                             (sb_su, su_ug),
                             (sb_sud, su_dp), (sb_wu, wsc_u), (sb_wg, wsc_g),
                             (sb_wd, wsc_dp)):
                c.dma("sync", dst[:, :], src[:, :], "sy", w=(CONST,))
            for buf, val in ((ones_c, 1.0), (ones_r, 1.0), (ebias, EBIAS)):
                c.op("vector", lambda E, buf=buf, val=val: E.memset(
                    buf[:, :], val), r=(CONST,), w=(CONST,))
            for j, val in ((0, RD), (1, EPS), (2, -1.0)):
                c.op("vector", lambda E, j=j, val=val: E.memset(
                    cst[:, j:j + 1], val), r=(CONST,), w=(CONST,))

            xb = [A(0, 2048), A(2048, 2048)]
            stg0 = A(4096, 8192)

            def p0_unit(mi, msrc, mdst, rb, cb):
                sgv = stg0.rearrange("p (cc r) -> p cc r", r=512)
                for rt in range(4):
                    r0 = rb * 512 + rt * 128
                    c.dma("sync", xb[rt % 2][:, :],
                          msrc[r0:r0 + 128, cb * 2048:(cb + 1) * 2048], "sy",
                          w=(("xb", rt % 2),))
                    for g in range(4):
                        bi = 6 + g % 2
                        c.grp([(lambda E, bi=bi, k=k, g=g, rt=rt:
                                E.transpose(
                                    pb[bi][:, k * 128:(k + 1) * 128],
                                    xb[rt % 2][:, (4 * g + k) * 128:
                                               (4 * g + k + 1) * 128],
                                    sb_i128[:, :]))
                               for k in range(4)],
                              r=(("xb", rt % 2), CONST), w=(("pb", bi),))
                        c.op("scalar", lambda E, bi=bi, g=g, rt=rt,
                             sgv=sgv: E.copy(
                                 sgv[:, 4 * g:4 * g + 4,
                                     rt * 128:(rt + 1) * 128], pb[bi][:, :]),
                             r=(("pb", bi),), w=(("stg",),))
                dv = mdst.rearrange("(cc p) r -> p cc r", p=128)[
                    :, cb * 16:(cb + 1) * 16, rb * 512:(rb + 1) * 512]
                c.dma("gpsimd", dv, sgv, "gp",
                      r=(("stg",),), w=(("wT", mi, rb, cb),))

            p0_units = []
            for (mi, (msrc, mdst, Rr, Cc)) in enumerate(
                    ((wup, wupT, INTER, DM), (wgate, wgT, INTER, DM),
                     (wdown, wdT, DM, INTER))):
                for rb in range(Rr // 512):
                    for cb in range(Cc // 2048):
                        p0_units.append((mi, msrc, mdst, rb, cb))

            wq_sb = A(12288, 12288)
            xt = [A(24576, 2048), A(26624, 2048)]
            xnT = A(28672, 8192)
            qkvstg = A(36864, 3072)
            vstg = A(39936, 512)
            sqscr = A(40448, 2048)
            wqv = wq_sb.rearrange("p (cc j) -> p cc j", j=768)
            c.dma("sync", wqv, wqkv.rearrange("(cc p) j -> p cc j", p=128),
                  "sy", w=(("wq",),))
            xnv = xnT.rearrange("p (cc t) -> p cc t", t=512)
            for s in range(NSTRIP):
                for tt in range(4):
                    t0 = s * 512 + tt * 128
                    XT = ("xt", tt % 2)
                    c.dma("sync", xt[tt % 2][:, :], x[t0:t0 + 128, :], "sy",
                          w=(XT,))
                    c.op("vector", lambda E, tt=tt: E.reduce_sum(
                        stat[:, 0:1], xt[tt % 2][:, :], axis=AX),
                        r=(XT,), w=(("st", 0),))
                    c.op("vector", lambda E: E.tensor_scalar(
                        stat[:, 1:2], stat[:, 0:1], cst[:, 0:1], None, MUL),
                        r=(("st", 0), CONST), w=(("st", 1),))
                    c.op("vector", lambda E: E.tensor_mul(
                        stat[:, 2:3], stat[:, 1:2], stat[:, 1:2]),
                        r=(("st", 1),), w=(("st", 2),))
                    c.op("vector", lambda E: E.tensor_scalar(
                        stat[:, 3:4], stat[:, 2:3], cst[:, 2:3], cst[:, 1:2],
                        MUL, ADD), r=(("st", 2), CONST), w=(("st", 3),))
                    c.op("scalar", lambda E, tt=tt: E.activation(
                        sqscr[:, :], xt[tt % 2][:, :], AF.Square,
                        accum_out=stat[:, 4:5]),
                        r=(XT,), w=(("sqscr",), ("st", 4)))
                    c.op("scalar", lambda E: E.activation(
                        stat[:, 5:6], stat[:, 4:5], AF.Sqrt,
                        bias=stat[:, 3:4], scale=RD),
                        r=(("st", 4), ("st", 3)), w=(("st", 5),))
                    c.op("vector", lambda E: E.reciprocal(
                        stat[:, 6:7], stat[:, 5:6]),
                        r=(("st", 5),), w=(("st", 6),))
                    c.op("vector", lambda E, tt=tt: E.tensor_scalar(
                        xt[tt % 2][:, :], xt[tt % 2][:, :],
                        stat[:, 1:2], stat[:, 6:7], SUB, MUL),
                        r=(XT, ("st", 1), ("st", 6)), w=(XT,))
                    for g in range(4):
                        bi = 6 + g % 2
                        c.grp([(lambda E, bi=bi, k=k, g=g, tt=tt: E.transpose(
                            pb[bi][:, k * 128:(k + 1) * 128],
                            xt[tt % 2][:, (4 * g + k) * 128:
                                       (4 * g + k + 1) * 128],
                            sb_i128[:, :])) for k in range(4)],
                            r=(XT, CONST), w=(("pb", bi),))
                        c.op("scalar", lambda E, bi=bi, g=g, tt=tt: E.copy(
                            xnv[:, 4 * g:4 * g + 4, tt * 128:(tt + 1) * 128],
                            pb[bi][:, :]),
                            r=(("pb", bi),), w=(("xnT", g, tt),))
                xnkeys = tuple(("xnT", g, tt) for g in range(4)
                               for tt in range(4))
                for jt in range(6):
                    c.grp([(lambda E, jt=jt, ccx=ccx: E.matmul(
                        pb[jt][:, 0:512],
                        wqv[:, ccx, jt * 128:(jt + 1) * 128],
                        xnv[:, ccx, :], start=(ccx == 0),
                        stop=(ccx == NCH - 1))) for ccx in range(NCH)],
                        r=xnkeys + (("wq",),), w=(("pb", jt),))
                for jt in range(6):
                    fn = (lambda E, jt=jt: E.mul(
                        qkvstg[:, jt * 512:(jt + 1) * 512], pb[jt][:, 0:512],
                        QSC)) if jt < 2 else (lambda E, jt=jt: E.copy(
                            qkvstg[:, jt * 512:(jt + 1) * 512],
                            pb[jt][:, 0:512]))
                    c.op("scalar", fn, r=(("pb", jt),), w=(("qkvstg", jt),))
                for h in range(2):
                    bi = 6 + h
                    c.grp([(lambda E, bi=bi, k=k, h=h: E.transpose(
                        pb[bi][:, k * 128:(k + 1) * 128],
                        qkvstg[:, (4 + h) * 512 + k * 128:
                               (4 + h) * 512 + (k + 1) * 128],
                        sb_i128[:, :])) for k in range(4)],
                        r=(("qkvstg", 4 + h), CONST), w=(("pb", bi),))
                    c.op("vector", lambda E, bi=bi: E.tensor_copy(
                        vstg[:, :], pb[bi][:, :]),
                        r=(("pb", bi),), w=(("vstg",),))
                    c.dma("gpsimd", vK[h, s, :, :], vstg[:, :], "gp",
                          r=(("vstg",),), w=(("vK", h, s),))
                for h in range(2):
                    c.dma("gpsimd", qT[h, :, s * 512:(s + 1) * 512],
                          qkvstg[:, h * 512:(h + 1) * 512], "gp",
                          r=(("qkvstg", h),), w=(("qT", h, s),))
                    c.dma("gpsimd", kT[h, :, s * 512:(s + 1) * 512],
                          qkvstg[:, (2 + h) * 512:(3 + h) * 512], "gp",
                          r=(("qkvstg", 2 + h),), w=(("kT", h, s),))
                for _ in range(3):
                    if p0_units:
                        p0_unit(*p0_units.pop(0))
            while p0_units:
                p0_unit(*p0_units.pop(0))
            c.barrier()

            ktb = A(0, 8192)
            vaug = A(8192, 8256)
            qb = [A(16448, 512), A(16960, 512)]
            pbf = [A(17472, 512), A(17984, 512)]
            ostg = [A(18496, 128), A(18624, 128)]
            vaugv = vaug.rearrange("p (k e) -> p k e", e=129)
            for h in range(2):
                kkeys = tuple(("kT", h, s2) for s2 in range(NSTRIP))
                vkeys = tuple(("vK", h, s2) for s2 in range(NSTRIP))
                c.dma("sync", ktb[:, :], kT[h, :, :], "sy",
                      r=kkeys, w=(("ktb",),))
                for s2 in range(NSTRIP):
                    c.dma("sync", vaugv[:, s2 * 4:(s2 + 1) * 4, 0:128],
                          vK[h, s2].rearrange("p (kb d) -> p kb d", d=128),
                          "sy", r=(("vK", h, s2),), w=(("vaug",),))
                c.op("vector", lambda E: E.memset(vaugv[:, :, 128:129], 1.0),
                     r=(), w=(("vaug1",),))
                for s in range(NSTRIP):
                    c.dma("sync", qb[s % 2][:, :],
                          qT[h, :, s * 512:(s + 1) * 512], "sy",
                          r=(("qT", h, s),), w=(("qb", s % 2),))
                    nkc = 4 * s + 4

                    def emit_s(kc, s=s):
                        sbk = kc % 2
                        diag = kc - 4 * s
                        c.grp([lambda E, sbk=sbk, kc=kc, s=s: E.matmul(
                            pb[sbk][:, 0:512],
                            ktb[:, kc * 128:(kc + 1) * 128],
                            qb[s % 2][:, :], start=True, stop=True)],
                            r=(("ktb",), ("qb", s % 2)), w=(("pb", sbk),))
                        o0 = 128 * diag if diag >= 0 else 0
                        c.op("scalar", lambda E, sbk=sbk, kc=kc, o0=o0:
                             E.activation(pbf[kc % 2][:, o0:512],
                                          pb[sbk][:, o0:512],
                                          AF.Exp, bias=ebias[:, 0:1]),
                             r=(("pb", sbk), CONST), w=(("pbf", kc % 2),))
                        if diag >= 0:
                            c.op("vector", lambda E, kc=kc, o0=o0:
                                 E.tensor_mul(
                                     pbf[kc % 2][:, o0:o0 + 128],
                                     pbf[kc % 2][:, o0:o0 + 128],
                                     sb_t128[:, :]),
                                 r=(("pbf", kc % 2), CONST),
                                 w=(("pbf", kc % 2),))

                    def emit_pv(kc, s=s):
                        diag = kc - 4 * s
                        t_lo = max(diag, 0)
                        for t in range(t_lo, 4):
                            stop = (kc == 4 * s + t)
                            c.grp([lambda E, t=t, kc=kc, stop=stop: E.matmul(
                                pb[2 + t][:, 0:129],
                                pbf[kc % 2][:, t * 128:(t + 1) * 128],
                                vaugv[:, kc, :],
                                start=(kc == 0), stop=stop)],
                                r=(("pbf", kc % 2), ("vaug",), ("vaug1",)),
                                w=(("pb", 2 + t),) if kc == 0 else
                                (("pbacc", 2 + t),))

                    emit_s(0)
                    for kc in range(1, nkc):
                        emit_s(kc)
                        emit_pv(kc - 1)
                    emit_pv(nkc - 1)
                    for t in range(4):
                        c.op("vector", lambda E, t=t: E.reciprocal(
                            stat[:, 8:9], pb[2 + t][:, 128:129]),
                            r=(("pb", 2 + t), ("pbacc", 2 + t)),
                            w=(("st", 8),))
                        c.op("vector", lambda E, t=t: E.tensor_scalar(
                            ostg[t % 2][:, :], pb[2 + t][:, 0:128],
                            stat[:, 8:9], None, MUL),
                            r=(("pb", 2 + t), ("pbacc", 2 + t), ("st", 8)),
                            w=(("ostg", t % 2),))
                        c.dma("gpsimd",
                              a2a_i[s * 512 + t * 128:s * 512 + (t + 1) * 128,
                                    h * 128:(h + 1) * 128],
                              ostg[t % 2][:, :], "gp",
                              r=(("ostg", t % 2),), w=(("a2a_i", h, s, t),))

            if c.eng == "gpsimd":
                c.wq("gp")
            with allow_ici_all_to_all():
                if c.eng == "gpsimd":
                    c.E.collective_compute(
                        "AllToAll", mybir.AluOpType.bypass,
                        replica_groups=[list(range(NCORE))],
                        ins=[a2a_i.opt()], outs=[a2a_o.opt()],
                    ).then_inc(c.sems["cc"], 1)
            c.cnt["cc"] += 1
            c._fin(("cc", c.cnt["cc"], "gpsimd"), (), (("a2a_o",),))
            if dbg:
                c.dma("gpsimd", dbg_a2a[:, :], a2a_o[:, :], "gp",
                      r=(("a2a_o",),), w=())
            c.barrier()

            attnT = A(0, 16384)
            yT = A(16384, 16384)
            wob = [A(32768, 2048), A(34816, 2048)]
            abuf = [A(36864, 128), A(36992, 128)]
            sq2 = A(37120, 1024)
            xt2 = A(38144, 2048)
            attv = attnT.rearrange("p (cc t) -> p cc t", t=1024)
            for ch in range(NCH):
                cb, col0 = ch // 2, (ch % 2) * 128
                for g in range(2):
                    bi = g % 2
                    for k in range(4):
                        tt = g * 4 + k
                        c.dma("sync", abuf[tt % 2][:, :],
                              a2a_o[cb * 1024 + tt * 128:
                                    cb * 1024 + (tt + 1) * 128,
                                    col0:col0 + 128], "sy",
                              r=(("a2a_o",),), w=(("abuf", tt % 2),))
                        c.grp([lambda E, bi=bi, k=k, tt=tt: E.transpose(
                            pb[bi][:, k * 128:(k + 1) * 128],
                            abuf[tt % 2][:, :], sb_i128[:, :])],
                            r=(("abuf", tt % 2), CONST), w=(("pb", bi),))
                    c.op("scalar", lambda E, bi=bi, ch=ch, g=g: E.copy(
                        attv[:, ch, g * 512:(g + 1) * 512], pb[bi][:, :]),
                        r=(("pb", bi),), w=(("attnT", ch),))

            woutv = wout.rearrange("(cc p) j -> p cc j", p=128)
            akeys = tuple(("attnT", ch) for ch in range(NCH))
            for jt in range(NCH):
                c.dma("sync",
                      wob[jt % 2].rearrange("p (cc j) -> p cc j", j=128),
                      woutv[:, :, jt * 128:(jt + 1) * 128], "sy",
                      w=(("wob", jt % 2),))
                wv = wob[jt % 2].rearrange("p (cc j) -> p cc j", j=128)
                for half in range(2):
                    bi = 2 * (jt % 2) + half
                    c.grp([(lambda E, bi=bi, wv=wv, ccx=ccx, half=half:
                            E.matmul(
                                pb[bi][:, 0:512], wv[:, ccx, :],
                                attv[:, ccx, half * 512:(half + 1) * 512],
                                start=(ccx == 0), stop=(ccx == NCH - 1)))
                           for ccx in range(NCH)],
                        r=akeys + (("wob", jt % 2),), w=(("pb", bi),))
                    c.op("scalar", lambda E, bi=bi, jt=jt, half=half: E.copy(
                        yT[:, jt * 1024 + half * 512:
                           jt * 1024 + (half + 1) * 512], pb[bi][:, :]),
                        r=(("pb", bi),), w=(("yT", jt),))

            def fm_ln(zread_fn, nch, width, sqbuf, sqkey, rowoff, banks,
                      apply_fn):
                b0, b1, b2, b3 = banks
                for ccx in range(nch):
                    ap, rk = zread_fn(ccx)
                    c.op("scalar", lambda E, ap=ap: E.activation(
                        sqbuf, ap, AF.Square), r=rk, w=(sqkey,))
                    c.grp([lambda E, ap=ap, ccx=ccx: E.matmul(
                        pb[b0][0:1, 0:width], ones_c[:, :], ap,
                        start=(ccx == 0), stop=(ccx == nch - 1))],
                        r=rk + (CONST,),
                        w=((("pb", b0),) if ccx == 0 else (("pbacc", b0),)))
                    c.grp([lambda E, ccx=ccx: E.matmul(
                        pb[b1][0:1, 0:width], ones_c[:, :], sqbuf,
                        start=(ccx == 0), stop=(ccx == nch - 1))],
                        r=(sqkey, CONST),
                        w=((("pb", b1),) if ccx == 0 else (("pbacc", b1),)))
                ro = rows[0:1, rowoff:rowoff + width]
                ro2 = rows[0:1, rowoff + width:rowoff + 2 * width]
                RO, RO2 = ("ro", rowoff), ("ro2", rowoff)
                PB0 = (("pb", b0), ("pbacc", b0))
                PB1 = (("pb", b1), ("pbacc", b1))
                c.op("vector", lambda E: E.tensor_scalar(
                    ro, pb[b0][0:1, 0:width], cst[0:1, 0:1], None, MUL),
                    r=PB0 + (CONST,), w=(RO,))
                c.op("vector", lambda E: E.tensor_scalar(
                    ro2, pb[b1][0:1, 0:width], cst[0:1, 0:1], None, MUL),
                    r=PB1 + (CONST,), w=(RO2,))
                c.op("vector", lambda E: E.tensor_mul(
                    pb[b0][0:1, 0:width], ro, ro),
                    r=(RO,) + PB0, w=(("pb", b0),))
                c.op("vector", lambda E: E.tensor_sub(
                    ro2, ro2, pb[b0][0:1, 0:width]),
                    r=(RO2, ("pb", b0)), w=(RO2,))
                c.op("vector", lambda E: E.tensor_scalar(
                    ro2, ro2, cst[0:1, 1:2], None, ADD),
                    r=(RO2, CONST), w=(RO2,))
                c.op("scalar", lambda E: E.activation(ro2, ro2, AF.Sqrt),
                     r=(RO2,), w=(RO2,))
                c.op("vector", lambda E: E.reciprocal(ro2, ro2),
                     r=(RO2,), w=(RO2,))
                c.op("vector", lambda E: E.tensor_mul(ro, ro, ro2),
                     r=(RO, RO2), w=(RO,))
                c.grp([lambda E: E.matmul(
                    pb[b2][:, 0:width], ones_r[0:1, :], ro2,
                    start=True, stop=True)], r=(RO2, CONST), w=(("pb", b2),))
                c.grp([lambda E: E.matmul(
                    pb[b3][:, 0:width], ones_r[0:1, :], ro,
                    start=True, stop=True)], r=(RO, CONST), w=(("pb", b3),))
                for ccx in range(nch):
                    apply_fn(ccx, pb[b2][:, 0:width], pb[b3][:, 0:width],
                             (("pb", b2), ("pb", b3)))

            for half in range(2):
                hoff = half * 512

                def yslice(ccx, hoff=hoff):
                    return yT[:, ccx * 1024 + hoff:ccx * 1024 + hoff + 512]

                def ap5(ccx, s_ap, ms_ap, bk):
                    c.op("vector", lambda E, ccx=ccx, s_ap=s_ap: E.tensor_mul(
                        yslice(ccx), yslice(ccx), s_ap),
                        r=(("yT", ccx),) + bk, w=(("yT", ccx),))
                    c.op("vector", lambda E, ccx=ccx, ms_ap=ms_ap:
                         E.tensor_sub(yslice(ccx), yslice(ccx), ms_ap),
                         r=(("yT", ccx),) + bk, w=(("yT", ccx),))
                fm_ln(lambda ccx: (yslice(ccx), (("yT", ccx),)),
                      NCH, 512, sq2[:, 0:512], ("sq2",),
                      1024 * half, (4, 5, 6, 7), ap5)
            yv1 = yT.rearrange("p (cc t) -> p cc t", t=1024)
            for tt in range(8):
                c.dma("sync", xt2[:, :], xsl[tt * 128:(tt + 1) * 128, :],
                      "sy", w=(("xt2",),))
                for g in range(4):
                    bi = g % 2
                    c.grp([(lambda E, bi=bi, k=k, g=g: E.transpose(
                        pb[bi][:, k * 128:(k + 1) * 128],
                        xt2[:, (4 * g + k) * 128:(4 * g + k + 1) * 128],
                        sb_i128[:, :])) for k in range(4)],
                        r=(("xt2",), CONST), w=(("pb", bi),))
                    ykeys = tuple(("yT", 4 * g + k) for k in range(4))
                    c.op("vector", lambda E, bi=bi, g=g, tt=tt:
                         E.tensor_tensor(
                             yv1[:, 4 * g:4 * g + 4, tt * 128:(tt + 1) * 128],
                             yv1[:, 4 * g:4 * g + 4, tt * 128:(tt + 1) * 128],
                             pb[bi][:, :], ADD),
                         r=ykeys + (("pb", bi),), w=ykeys)
            c.dma("gpsimd", rT.rearrange("(cc p) t -> p cc t", p=128),
                  yT.rearrange("p (cc t) -> p cc t", t=1024), "gp",
                  r=tuple(("yT", j) for j in range(NCH)), w=(("rT",),))
            c.barrier()

            rsl = A(0, 4096)
            zb = A(4096, 4096)
            h2 = A(8192, 16384)
            wb = [A(24576, 2048), A(26624, 2048),
                  A(28672, 2048), A(30720, 2048)]
            hh = [A(32768, 256), A(42496, 256)]
            sq6 = [A(33024, 256), A(42752, 256)]
            tmp6 = A(33280, 256)
            ostg2 = A(33536, 2048)
            z1 = A(35584, 4096)
            rslv = rsl.rearrange("p (cc t) -> p cc t", t=FT)
            zv = zb.rearrange("p (cc t) -> p cc t", t=FT)
            z1v = z1.rearrange("p (cc t) -> p cc t", t=FT)
            y2v = zb.rearrange("p (cc t) -> p cc t", t=FT)
            h2v = h2.rearrange("p (ic t) -> p ic t", t=FT)
            wupTv = wupT.rearrange("(cc p) i -> p cc i", p=128)
            wgTv = wgT.rearrange("(cc p) i -> p cc i", p=128)
            wdTv = wdT.rearrange("(ic p) j -> p ic j", p=128)
            rTv = rT.rearrange("(cc p) t -> p cc t", p=128)
            zbkeys = tuple(("zb", ccx) for ccx in range(NCH))

            for sub in range(NSUB):
                toff = sub * FT
                c.dma("sync", rslv, rTv[:, :, toff:toff + FT], "sy",
                      r=(("rT",),), w=(("rsl",),))

                def apz(ccx, s_ap, ms_ap, bk):
                    c.op("vector", lambda E, ccx=ccx, s_ap=s_ap:
                         E.tensor_tensor(zv[:, ccx, :], rslv[:, ccx, :],
                                         s_ap, MUL),
                         r=(("rsl",),) + bk, w=(("zb", ccx),))
                    c.op("vector", lambda E, ccx=ccx, ms_ap=ms_ap:
                         E.tensor_sub(zv[:, ccx, :], zv[:, ccx, :], ms_ap),
                         r=(("zb", ccx),) + bk, w=(("zb", ccx),))
                    c.op("vector", lambda E, ccx=ccx: E.tensor_scalar(
                        zv[:, ccx, :], zv[:, ccx, :], sb_su[:, ccx:ccx + 1],
                        None, MUL), r=(("zb", ccx), CONST), w=(("zb", ccx),))
                fm_ln(lambda ccx: (rslv[:, ccx, :], (("rsl",),)),
                      NCH, FT, sq6[0][:, :], ("sq6", 0), 2048,
                      (0, 1, 2, 3), apz)
                for ccx in range(NCH):
                    bi = 4 + ccx % 2
                    c.grp([lambda E, bi=bi, ccx=ccx: E.matmul(
                        pb[bi][:, 0:FT], sb_h128a[:, :], zv[:, ccx, :],
                        start=True, stop=True)],
                        r=(("zb", ccx), CONST), w=(("pb", bi),))
                    c.op("scalar", lambda E, bi=bi, ccx=ccx: E.copy(
                        z1v[:, ccx, :], pb[bi][:, 0:FT]),
                        r=(("pb", bi),), w=(("z1", ccx),))
                for st in range(4):
                    bit = 1 << st
                    for i in range(NCH):
                        if i & bit:
                            continue
                        j = i | bit
                        c.op("vector", lambda E, i=i, j=j: E.tensor_sub(
                            tmp6[:, :], z1v[:, i, :], z1v[:, j, :]),
                            r=(("z1", i), ("z1", j)), w=(("tmp6",),))
                        c.op("vector", lambda E, i=i, j=j: E.tensor_add(
                            z1v[:, i, :], z1v[:, i, :], z1v[:, j, :]),
                            r=(("z1", i), ("z1", j)), w=(("z1", i),))
                        c.op("vector", lambda E, j=j: E.tensor_copy(
                            z1v[:, j, :], tmp6[:, :]),
                            r=(("tmp6",),), w=(("z1", j),))
                z1keys = tuple(("z1", ccx) for ccx in range(NCH))
                def emit_h2(ic):
                    bh = 4 + ic % 2
                    c.grp([lambda E, bh=bh, ic=ic: E.matmul(
                        pb[bh][:, 0:FT], sb_h128b[:, :], hh[ic % 2][:, :],
                        start=True, stop=True)],
                        r=(("hh", ic % 2), CONST), w=(("pb", bh),))
                    c.op("scalar", lambda E, bh=bh, ic=ic: E.copy(
                        h2v[:, ic, :], pb[bh][:, 0:FT]),
                        r=(("pb", bh),), w=(("h2", ic),))

                for ic in range(NIC):
                    c.dma("sync",
                          wb[ic % 2].rearrange("p (cc j) -> p cc j", j=128),
                          wupTv[:, :, ic * 128:(ic + 1) * 128], "sy",
                          r=(("wT", 0, ic // 4, 0),), w=(("wb", ic % 2),))
                    c.dma("sync",
                          wb[2 + ic % 2].rearrange("p (cc j) -> p cc j",
                                                   j=128),
                          wgTv[:, :, ic * 128:(ic + 1) * 128], "sy",
                          r=(("wT", 1, ic // 4, 0),), w=(("wb", 2 + ic % 2),))
                    buv = wb[ic % 2].rearrange("p (cc j) -> p cc j", j=128)
                    bgv = wb[2 + ic % 2].rearrange("p (cc j) -> p cc j",
                                                   j=128)
                    bu, bg = ic % 2, 2 + ic % 2
                    c.grp([(lambda E, bu=bu, buv=buv, ccx=ccx: E.matmul(
                        pb[bu][:, 0:FT], buv[:, ccx, :], z1v[:, ccx, :],
                        start=(ccx == 0), stop=(ccx == NCH - 1)))
                        for ccx in range(NCH)],
                        r=z1keys + (("wb", bu),), w=(("pb", bu),))
                    c.grp([(lambda E, bg=bg, bgv=bgv, ccx=ccx: E.matmul(
                        pb[bg][:, 0:FT], bgv[:, ccx, :], z1v[:, ccx, :],
                        start=(ccx == 0), stop=(ccx == NCH - 1)))
                        for ccx in range(NCH)],
                        r=z1keys + (("wb", bg),), w=(("pb", bg),))
                    c.op("scalar", lambda E, bg=bg, ic=ic: E.activation(
                        sq6[ic % 2][:, :], pb[bg][:, 0:FT], AF.Silu,
                        scale=sb_wg[:, ic:ic + 1]),
                        r=(("pb", bg), CONST), w=(("sq6", ic % 2),))
                    c.op("vector", lambda E, bu=bu, ic=ic: E.tensor_scalar(
                        hh[ic % 2][:, :], pb[bu][:, 0:FT],
                        sb_wu[:, ic:ic + 1], sb_sud[:, ic:ic + 1], MUL, MUL),
                        r=(("pb", bu), CONST), w=(("hh", ic % 2),))
                    c.op("vector", lambda E, ic=ic: E.tensor_mul(
                        hh[ic % 2][:, :], hh[ic % 2][:, :],
                        sq6[ic % 2][:, :]),
                        r=(("hh", ic % 2), ("sq6", ic % 2)),
                        w=(("hh", ic % 2),))
                    if ic > 0:
                        emit_h2(ic - 1)
                emit_h2(NIC - 1)
                for st in range(6):
                    bit = 1 << st
                    for i in range(NIC):
                        if i & bit:
                            continue
                        j = i | bit
                        c.op("vector", lambda E, i=i, j=j: E.tensor_sub(
                            tmp6[:, :], h2v[:, i, :], h2v[:, j, :]),
                            r=(("h2", i), ("h2", j)), w=(("tmp6",),))
                        c.op("vector", lambda E, i=i, j=j: E.tensor_add(
                            h2v[:, i, :], h2v[:, i, :], h2v[:, j, :]),
                            r=(("h2", i), ("h2", j)), w=(("h2", i),))
                        c.op("vector", lambda E, j=j: E.tensor_copy(
                            h2v[:, j, :], tmp6[:, :]),
                            r=(("tmp6",),), w=(("h2", j),))
                h2keys = tuple(("h2", ic) for ic in range(NIC))
                for jt in range(NCH):
                    bd = 6 + jt % 2
                    for q in range(4):
                        c.dma("sync",
                              wb[q % 2].rearrange("p (ic j) -> p ic j",
                                                  j=128),
                              wdTv[:, q * 16:(q + 1) * 16,
                                   jt * 128:(jt + 1) * 128], "sy",
                              r=(("wT", 2, jt // 4, q),),
                              w=(("wb", q % 2),))
                        wdv = wb[q % 2].rearrange("p (ic j) -> p ic j", j=128)
                        c.grp([(lambda E, bd=bd, wdv=wdv, icq=icq, q=q:
                                E.matmul(
                                    pb[bd][:, 0:FT], wdv[:, icq, :],
                                    h2v[:, q * 16 + icq, :],
                                    start=(q * 16 + icq == 0),
                                    stop=(q * 16 + icq == NIC - 1)))
                               for icq in range(16)],
                            r=h2keys + (("wb", q % 2),),
                            w=((("pb", bd),) if q == 0 else
                               (("pbacc", bd),)))
                    c.op("vector", lambda E, jt=jt, bd=bd: E.tensor_scalar(
                        y2v[:, jt, :], pb[bd][:, 0:FT],
                        sb_wd[:, jt:jt + 1], None, MUL),
                        r=(("pb", bd), ("pbacc", bd), CONST),
                        w=(("zb", jt),))

                def apo(ccx, s_ap, ms_ap, bk):
                    c.op("vector", lambda E, ccx=ccx, s_ap=s_ap: E.tensor_mul(
                        y2v[:, ccx, :], y2v[:, ccx, :], s_ap),
                        r=(("zb", ccx),) + bk, w=(("zb", ccx),))
                    c.op("vector", lambda E, ccx=ccx, ms_ap=ms_ap:
                         E.tensor_sub(y2v[:, ccx, :], y2v[:, ccx, :], ms_ap),
                         r=(("zb", ccx),) + bk, w=(("zb", ccx),))
                    c.op("vector", lambda E, ccx=ccx: E.tensor_tensor(
                        y2v[:, ccx, :], y2v[:, ccx, :], rslv[:, ccx, :],
                        ADD), r=(("zb", ccx), ("rsl",)), w=(("zb", ccx),))
                fm_ln(lambda ccx: (y2v[:, ccx, :], (("zb", ccx),)),
                      NCH, FT, sq6[0][:, :], ("sq6", 0), 2048,
                      (0, 1, 2, 3), apo)
                for tt2 in range(2):
                    for g in range(4):
                        bi = 4 + g % 2
                        zkeys = tuple(("zb", 4 * g + k) for k in range(4))
                        c.grp([(lambda E, bi=bi, k=k, g=g, tt2=tt2:
                                E.transpose(
                                    pb[bi][:, k * 128:(k + 1) * 128],
                                    y2v[:, 4 * g + k,
                                        tt2 * 128:(tt2 + 1) * 128],
                                    sb_i128[:, :])) for k in range(4)],
                            r=zkeys + (CONST,), w=(("pb", bi),))
                        c.op("scalar", lambda E, bi=bi, g=g: E.copy(
                            ostg2[:, g * 512:(g + 1) * 512], pb[bi][:, :]),
                            r=(("pb", bi),), w=(("ostg2", g),))
                    c.dma("gpsimd",
                          out[toff + tt2 * 128:toff + (tt2 + 1) * 128, :],
                          ostg2[:, :], "gp",
                          r=tuple(("ostg2", g) for g in range(4)), w=())
            if dbg == 2:
                c.barrier()
                for name, (src, dst) in dbg_out.items():
                    nd = len(src.tensor.shape)
                    sl = (slice(None),) * nd
                    c.dma("gpsimd", dst[sl], src[sl], "gp")
            if c.eng == "gpsimd":
                c.wq("gp")

        with nc.Block() as block:
            @block.sync
            def _(e):
                emit(Ctx("sync", e, sems))

            @block.tensor
            def _(e):
                emit(Ctx("tensor", e, sems))

            @block.vector
            def _(e):
                emit(Ctx("vector", e, sems))

            @block.scalar
            def _(e):
                emit(Ctx("scalar", e, sems))

            @block.gpsimd
            def _(e):
                emit(Ctx("gpsimd", e, sems))

    return nc


def kernel(L0_attn_x, L0_attn_w_qkv, L0_attn_w_out, L0_ffn_SU_ug, L0_ffn_SU_dp,
           L0_ffn_Wscale_ug, L0_ffn_Wscale_dp, L0_ffn_W_up, L0_ffn_W_gate,
           L0_ffn_W_down):
    cont = np.ascontiguousarray
    x = cont(np.asarray(L0_attn_x, dtype=np.float32).reshape(SEQ, DM))
    wqkv = np.asarray(L0_attn_w_qkv, dtype=np.float32)
    base = {
        "x": x,
        "wout": cont(np.asarray(L0_attn_w_out, dtype=np.float32)),
        "wup": cont(np.asarray(L0_ffn_W_up, dtype=np.float32)),
        "wgate": cont(np.asarray(L0_ffn_W_gate, dtype=np.float32)),
        "wdown": cont(np.asarray(L0_ffn_W_down, dtype=np.float32)),
        "su_ug": cont(np.asarray(L0_ffn_SU_ug, dtype=np.float32)
                      .reshape(NCH, 128).T),
        "su_dp": cont(np.asarray(L0_ffn_SU_dp, dtype=np.float32)
                      .reshape(NIC, 128).T),
        "wsc_u": cont(np.asarray(L0_ffn_Wscale_ug, dtype=np.float32)[:INTER]
                      .reshape(NIC, 128).T),
        "wsc_g": cont(np.asarray(L0_ffn_Wscale_ug, dtype=np.float32)[INTER:]
                      .reshape(NIC, 128).T),
        "wsc_dp": cont(np.asarray(L0_ffn_Wscale_dp, dtype=np.float32)
                       .reshape(NCH, 128).T),
        "i128": np.eye(128, dtype=np.float32),
        "t128": (np.arange(128)[None, :] >=
                 np.arange(128)[:, None]).astype(np.float32),
        "h128a": hadamard(128) / np.float32(np.sqrt(2048.0)),
        "h128b": hadamard(128) / np.float32(np.sqrt(8192.0)),
    }
    nc = build_nc()
    in_maps = []
    for cidx in range(NCORE):
        h0, h1 = 2 * cidx, 2 * cidx + 1
        wq_s = cont(np.concatenate(
            [wqkv[:, h0 * HD:(h0 + 1) * HD],
             wqkv[:, h1 * HD:(h1 + 1) * HD],
             wqkv[:, DM + h0 * HD:DM + (h0 + 1) * HD],
             wqkv[:, DM + h1 * HD:DM + (h1 + 1) * HD],
             wqkv[:, 2 * DM + h0 * HD:2 * DM + (h0 + 1) * HD],
             wqkv[:, 2 * DM + h1 * HD:2 * DM + (h1 + 1) * HD]], axis=1))
        m = dict(base)
        m["wqkv"] = wq_s
        m["xsl"] = cont(x[cidx * TOK:(cidx + 1) * TOK, :])
        in_maps.append(m)
    res = run_bass_kernel_spmd(nc, in_maps, core_ids=list(range(NCORE)))
    outp = np.concatenate([res.results[cidx]["out"] for cidx in range(NCORE)],
                          axis=0)
    return outp.reshape(1, SEQ, DM)


# version 10
# speedup vs baseline: 1.2244x; 1.0889x over previous
import contextlib
import numpy as np

import concourse.bass as bass
import concourse.mybir as mybir
from concourse.bass_utils import run_bass_kernel_spmd
from concourse.aot_env import allow_ici_all_to_all

F32 = mybir.dt.float32
BF16 = mybir.dt.bfloat16
AX = mybir.AxisListType.X
ADD = mybir.AluOpType.add
SUB = mybir.AluOpType.subtract
MUL = mybir.AluOpType.mult
AF = mybir.ActivationFunctionType

SEQ, DM, NH, HD, NCORE = 8192, 2048, 16, 128, 8
HPC = NH // NCORE
TOK = SEQ // NCORE
INTER = 8192
QS = 512
NSTRIP = SEQ // QS
FT = 256
NSUB = TOK // FT
NCH = DM // 128
NIC = INTER // 128
EPS = 1e-5
EBIAS = -20.0
QSC = float(1.0 / np.sqrt(128.0))
RD = float(1.0 / DM)

OWN = {"tensor": "te", "vector": "ve", "scalar": "sc", "gpsimd": "gc"}
NSY = 8
NGP = 4
CONST = ("const",)


class Ctx:

    def __init__(self, eng, E, sems):
        self.eng = eng
        self.E = E
        self.sems = sems
        self.cnt = {k: 0 for k in sems}
        self._lastw = {}
        self._rr = {"sy": 0, "gp": 0}
        self.last_w = {}
        self.readers = {}

    def w(self, sem, val):
        if val is None or val <= 0:
            return
        if self._lastw.get(sem, -1) >= val:
            return
        self._lastw[sem] = val
        if self.E is not None:
            self.E.wait_ge(self.sems[sem], val)

    def _dep(self, eng, reads, writes):
        emitting = (self.eng == eng)
        for k in reads:
            t = self.last_w.get(k)
            if t is not None and not (eng == "tensor" and t[2] == "tensor"):
                if emitting:
                    self.w(t[0], t[1])
        for k in writes:
            t = self.last_w.get(k)
            if t is not None and not (eng == "tensor" and t[2] == "tensor"):
                if emitting:
                    self.w(t[0], t[1])
            for rt in self.readers.get(k, ()):
                if eng == "tensor" and rt[2] == "tensor":
                    continue
                if emitting:
                    self.w(rt[0], rt[1])

    def _fin(self, tok, reads, writes):
        for k in writes:
            self.last_w[k] = tok
            self.readers[k] = []
        for k in reads:
            self.readers.setdefault(k, []).append(tok)

    def op(self, eng, fn, r=(), w=()):
        self._dep(eng, r, w)
        sem = OWN[eng]
        if self.eng == eng:
            fn(self.E).then_inc(self.sems[sem], 1)
        self.cnt[sem] += 1
        tok = (sem, self.cnt[sem], eng)
        self._fin(tok, r, w)
        return tok

    def grp(self, fns, r=(), w=()):
        self._dep("tensor", r, w)
        n = len(fns)
        if self.eng == "tensor":
            for i, fn in enumerate(fns):
                ins = fn(self.E)
                if i == n - 1:
                    ins.then_inc(self.sems["te"], 1)
        self.cnt["te"] += 1
        tok = ("te", self.cnt["te"], "tensor")
        self._fin(tok, r, w)
        return tok

    def dma(self, eng, out, in_, q, r=(), w=()):
        self._dep(eng, r, w)
        n = NSY if q == "sy" else NGP
        lane = self._rr[q]
        self._rr[q] = (lane + 1) % n
        sem = (q, lane)
        if self.eng == eng:
            self.w(sem, self.cnt[sem])
            self.E.dma_start(out=out, in_=in_).then_inc(self.sems[sem], 16)
        self.cnt[sem] += 16
        tok = (sem, self.cnt[sem], eng)
        self._fin(tok, r, w)
        return tok

    def wq(self, q):
        n = NSY if q == "sy" else NGP
        for lane in range(n):
            self.w((q, lane), self.cnt[(q, lane)])

    def barrier(self):
        for sem in ("te", "ve", "sc", "gc", "cc"):
            self.w(sem, self.cnt[sem])
        self.wq("sy")
        self.wq("gp")


def hadamard(n):
    h = np.array([[1.0]], dtype=np.float32)
    while h.shape[0] < n:
        h = np.block([[h, h], [h, -h]])
    return h.astype(np.float32)


def build_nc(dbg=False):
    nc = bass.Bass("TRN2", target_bir_lowering=False)

    def din(name, shape):
        return nc.dram_tensor(name, shape, F32, kind="ExternalInput").ap()

    x = din("x", [SEQ, DM])
    xsl = din("xsl", [TOK, DM])
    wqkv = din("wqkv", [DM, 6 * HD])
    wout = din("wout", [DM, DM])
    wup = din("wup", [INTER, DM])
    wgate = din("wgate", [INTER, DM])
    wdown = din("wdown", [DM, INTER])
    su_ug = din("su_ug", [128, NCH])
    su_dp = din("su_dp", [128, NIC])
    wsc_u = din("wsc_u", [128, NIC])
    wsc_g = din("wsc_g", [128, NIC])
    wsc_dp = din("wsc_dp", [128, NCH])
    i128 = din("i128", [128, 128])
    t128 = din("t128", [128, 128])
    h128a = din("h128a", [128, 128])
    h128b = din("h128b", [128, 128])

    out = nc.dram_tensor("out", [TOK, DM], F32, kind="ExternalOutput").ap()

    dbg_out = {}

    def scr(name, shape):
        kind = "ExternalOutput" if dbg == 1 else "Internal"
        ap = nc.dram_tensor(name, shape, F32, kind=kind).ap()
        if dbg == 2:
            dbg_out[name] = (ap, nc.dram_tensor(
                "dbg_" + name, shape, F32, kind="ExternalOutput").ap())
        return ap

    wupH = nc.dram_tensor("wupH", [DM, INTER], BF16, kind="Internal").ap()
    wupL = nc.dram_tensor("wupL", [DM, INTER], BF16, kind="Internal").ap()
    wgH = nc.dram_tensor("wgH", [DM, INTER], BF16, kind="Internal").ap()
    wgL = nc.dram_tensor("wgL", [DM, INTER], BF16, kind="Internal").ap()
    wdT = scr("wdT", [INTER, DM])
    qT = scr("qTs", [HPC, HD, SEQ])
    kT = scr("kTs", [HPC, HD, SEQ])
    vK = scr("vKs", [HPC, NSTRIP, 128, 512])
    rT = scr("rTs", [DM, TOK])
    a2a_i = nc.dram_tensor("a2a_i", [SEQ, HPC * HD], F32,
                           kind="Internal").ap()
    a2a_o = nc.dram_tensor("a2a_o", [SEQ, HPC * HD], F32,
                           kind="Internal").ap()
    dbg_a2a = (nc.dram_tensor("dbg_a2a", [SEQ, HPC * HD], F32,
                              kind="ExternalOutput").ap() if dbg else None)

    es = contextlib.ExitStack()
    with es:
        AR = 43008
        arena = es.enter_context(nc.sbuf_tensor("arena", [128, AR], F32))
        sb_i128 = es.enter_context(nc.sbuf_tensor("sb_i128", [128, 128], F32))
        sb_t128 = es.enter_context(nc.sbuf_tensor("sb_t128", [128, 128], F32))
        sb_h128a = es.enter_context(
            nc.sbuf_tensor("sb_h128a", [128, 128], F32))
        sb_h128b = es.enter_context(
            nc.sbuf_tensor("sb_h128b", [128, 128], F32))
        ones_c = es.enter_context(nc.sbuf_tensor("ones_c", [128, 1], F32))
        ones_r = es.enter_context(nc.sbuf_tensor("ones_r", [1, 128], F32))
        sb_su = es.enter_context(nc.sbuf_tensor("sb_su", [128, NCH], F32))
        sb_sud = es.enter_context(nc.sbuf_tensor("sb_sud", [128, NIC], F32))
        sb_wu = es.enter_context(nc.sbuf_tensor("sb_wu", [128, NIC], F32))
        sb_wg = es.enter_context(nc.sbuf_tensor("sb_wg", [128, NIC], F32))
        sb_wd = es.enter_context(nc.sbuf_tensor("sb_wd", [128, NCH], F32))
        stat = es.enter_context(nc.sbuf_tensor("stat", [128, 16], F32))
        ebias = es.enter_context(nc.sbuf_tensor("ebias", [128, 1], F32))
        cst = es.enter_context(nc.sbuf_tensor("cst", [128, 3], F32))
        rows = es.enter_context(nc.sbuf_tensor("rows", [1, 4096], F32))
        sgh_t = es.enter_context(nc.sbuf_tensor("sgh_t", [128, 4096], BF16))
        sgl_t = es.enter_context(nc.sbuf_tensor("sgl_t", [128, 4096], BF16))

        pb = [es.enter_context(nc.psum_tensor(f"pb{i}", [128, 512], F32))
              for i in range(8)]

        sems = {}
        for sname in ("te", "ve", "sc", "gc", "cc"):
            sems[sname] = es.enter_context(nc.semaphore(sname))
        for lane in range(NSY):
            sems[("sy", lane)] = es.enter_context(nc.semaphore(f"sy{lane}"))
        for lane in range(NGP):
            sems[("gp", lane)] = es.enter_context(nc.semaphore(f"gp{lane}"))

        def A(off, span):
            return arena[:, off:off + span]

        def emit(c: Ctx):
            for dst, src in ((sb_i128, i128), (sb_t128, t128),
                             (sb_h128a, h128a), (sb_h128b, h128b),
                             (sb_su, su_ug),
                             (sb_sud, su_dp), (sb_wu, wsc_u), (sb_wg, wsc_g),
                             (sb_wd, wsc_dp)):
                c.dma("sync", dst[:, :], src[:, :], "sy", w=(CONST,))
            for buf, val in ((ones_c, 1.0), (ones_r, 1.0), (ebias, EBIAS)):
                c.op("vector", lambda E, buf=buf, val=val: E.memset(
                    buf[:, :], val), r=(CONST,), w=(CONST,))
            for j, val in ((0, RD), (1, EPS), (2, -1.0)):
                c.op("vector", lambda E, j=j, val=val: E.memset(
                    cst[:, j:j + 1], val), r=(CONST,), w=(CONST,))

            xb = [A(0, 2048), A(2048, 2048)]
            stg0 = A(4096, 8192)

            sghv = sgh_t[:, :].rearrange("p (cc r) -> p cc r", r=256)
            sglv = sgl_t[:, :].rearrange("p (cc r) -> p cc r", r=256)

            def p0_unit(mi, msrc, mdst, rb, cb):
                sgv = stg0.rearrange("p (cc r) -> p cc r", r=512)
                split = mi < 2
                for half in range(2):
                    for rt in (2 * half, 2 * half + 1):
                        r0 = rb * 512 + rt * 128
                        c.dma("sync", xb[rt % 2][:, :],
                              msrc[r0:r0 + 128, cb * 2048:(cb + 1) * 2048],
                              "sy", w=(("xb", rt % 2),))
                        for g in range(4):
                            bi = 6 + g % 2
                            c.grp([(lambda E, bi=bi, k=k, g=g, rt=rt:
                                    E.transpose(
                                        pb[bi][:, k * 128:(k + 1) * 128],
                                        xb[rt % 2][:, (4 * g + k) * 128:
                                                   (4 * g + k + 1) * 128],
                                        sb_i128[:, :]))
                                   for k in range(4)],
                                  r=(("xb", rt % 2), CONST), w=(("pb", bi),))
                            rr = (rt - 2 * half) * 128
                            if split:
                                hsl = sghv[:, 4 * g:4 * g + 4, rr:rr + 128]
                                lsl = sglv[:, 4 * g:4 * g + 4, rr:rr + 128]
                                c.op("scalar", lambda E, bi=bi, hsl=hsl:
                                     E.copy(hsl, pb[bi][:, :].rearrange(
                                         "p (cc r) -> p cc r", r=128)),
                                     r=(("pb", bi),), w=(("stgh",),))
                                c.op("vector",
                                     lambda E, bi=bi, hsl=hsl, lsl=lsl:
                                     E.tensor_sub(
                                         lsl, pb[bi][:, :].rearrange(
                                             "p (cc r) -> p cc r", r=128),
                                         hsl),
                                     r=(("pb", bi), ("stgh",)),
                                     w=(("stgl",),))
                            else:
                                c.op("scalar", lambda E, bi=bi, g=g, rt=rt,
                                     sgv=sgv: E.copy(
                                         sgv[:, 4 * g:4 * g + 4,
                                             rt * 128:(rt + 1) * 128],
                                         pb[bi][:, :]),
                                     r=(("pb", bi),), w=(("stg",),))
                    if split:
                        mh, ml = (wupH, wupL) if mi == 0 else (wgH, wgL)
                        for tgt, view, key in ((mh, sghv, ("stgh",)),
                                               (ml, sglv, ("stgl",))):
                            dvb = tgt.rearrange("(cc p) r -> p cc r", p=128)[
                                :, cb * 16:(cb + 1) * 16,
                                rb * 512 + half * 256:
                                rb * 512 + half * 256 + 256]
                            c.dma("gpsimd", dvb, view, "gp",
                                  r=(key,),
                                  w=(("wT", mi, rb, cb, key[0], half),))
                if not split:
                    dv = mdst.rearrange("(cc p) r -> p cc r", p=128)[
                        :, cb * 16:(cb + 1) * 16, rb * 512:(rb + 1) * 512]
                    c.dma("gpsimd", dv, sgv, "gp",
                          r=(("stg",),), w=(("wT", mi, rb, cb),))

            p0_units = []
            for (mi, (msrc, mdst, Rr, Cc)) in enumerate(
                    ((wup, None, INTER, DM), (wgate, None, INTER, DM),
                     (wdown, wdT, DM, INTER))):
                for rb in range(Rr // 512):
                    for cb in range(Cc // 2048):
                        p0_units.append((mi, msrc, mdst, rb, cb))

            wq_sb = A(12288, 12288)
            xt = [A(24576, 2048), A(26624, 2048)]
            xnT = A(28672, 8192)
            qkvstg = A(36864, 3072)
            vstg = A(39936, 512)
            sqscr = A(40448, 2048)
            wqv = wq_sb.rearrange("p (cc j) -> p cc j", j=768)
            c.dma("sync", wqv, wqkv.rearrange("(cc p) j -> p cc j", p=128),
                  "sy", w=(("wq",),))
            xnv = xnT.rearrange("p (cc t) -> p cc t", t=512)
            for s in range(NSTRIP):
                for tt in range(4):
                    t0 = s * 512 + tt * 128
                    XT = ("xt", tt % 2)
                    c.dma("sync", xt[tt % 2][:, :], x[t0:t0 + 128, :], "sy",
                          w=(XT,))
                    c.op("vector", lambda E, tt=tt: E.reduce_sum(
                        stat[:, 0:1], xt[tt % 2][:, :], axis=AX),
                        r=(XT,), w=(("st", 0),))
                    c.op("vector", lambda E: E.tensor_scalar(
                        stat[:, 1:2], stat[:, 0:1], cst[:, 0:1], None, MUL),
                        r=(("st", 0), CONST), w=(("st", 1),))
                    c.op("vector", lambda E: E.tensor_mul(
                        stat[:, 2:3], stat[:, 1:2], stat[:, 1:2]),
                        r=(("st", 1),), w=(("st", 2),))
                    c.op("vector", lambda E: E.tensor_scalar(
                        stat[:, 3:4], stat[:, 2:3], cst[:, 2:3], cst[:, 1:2],
                        MUL, ADD), r=(("st", 2), CONST), w=(("st", 3),))
                    c.op("scalar", lambda E, tt=tt: E.activation(
                        sqscr[:, :], xt[tt % 2][:, :], AF.Square,
                        accum_out=stat[:, 4:5]),
                        r=(XT,), w=(("sqscr",), ("st", 4)))
                    c.op("scalar", lambda E: E.activation(
                        stat[:, 5:6], stat[:, 4:5], AF.Sqrt,
                        bias=stat[:, 3:4], scale=RD),
                        r=(("st", 4), ("st", 3)), w=(("st", 5),))
                    c.op("vector", lambda E: E.reciprocal(
                        stat[:, 6:7], stat[:, 5:6]),
                        r=(("st", 5),), w=(("st", 6),))
                    c.op("vector", lambda E, tt=tt: E.tensor_scalar(
                        xt[tt % 2][:, :], xt[tt % 2][:, :],
                        stat[:, 1:2], stat[:, 6:7], SUB, MUL),
                        r=(XT, ("st", 1), ("st", 6)), w=(XT,))
                    for g in range(4):
                        bi = 6 + g % 2
                        c.grp([(lambda E, bi=bi, k=k, g=g, tt=tt: E.transpose(
                            pb[bi][:, k * 128:(k + 1) * 128],
                            xt[tt % 2][:, (4 * g + k) * 128:
                                       (4 * g + k + 1) * 128],
                            sb_i128[:, :])) for k in range(4)],
                            r=(XT, CONST), w=(("pb", bi),))
                        c.op("scalar", lambda E, bi=bi, g=g, tt=tt: E.copy(
                            xnv[:, 4 * g:4 * g + 4, tt * 128:(tt + 1) * 128],
                            pb[bi][:, :]),
                            r=(("pb", bi),), w=(("xnT", g, tt),))
                xnkeys = tuple(("xnT", g, tt) for g in range(4)
                               for tt in range(4))
                for jt in range(6):
                    c.grp([(lambda E, jt=jt, ccx=ccx: E.matmul(
                        pb[jt][:, 0:512],
                        wqv[:, ccx, jt * 128:(jt + 1) * 128],
                        xnv[:, ccx, :], start=(ccx == 0),
                        stop=(ccx == NCH - 1))) for ccx in range(NCH)],
                        r=xnkeys + (("wq",),), w=(("pb", jt),))
                for jt in range(6):
                    fn = (lambda E, jt=jt: E.mul(
                        qkvstg[:, jt * 512:(jt + 1) * 512], pb[jt][:, 0:512],
                        QSC)) if jt < 2 else (lambda E, jt=jt: E.copy(
                            qkvstg[:, jt * 512:(jt + 1) * 512],
                            pb[jt][:, 0:512]))
                    c.op("scalar", fn, r=(("pb", jt),), w=(("qkvstg", jt),))
                for h in range(2):
                    bi = 6 + h
                    c.grp([(lambda E, bi=bi, k=k, h=h: E.transpose(
                        pb[bi][:, k * 128:(k + 1) * 128],
                        qkvstg[:, (4 + h) * 512 + k * 128:
                               (4 + h) * 512 + (k + 1) * 128],
                        sb_i128[:, :])) for k in range(4)],
                        r=(("qkvstg", 4 + h), CONST), w=(("pb", bi),))
                    c.op("vector", lambda E, bi=bi: E.tensor_copy(
                        vstg[:, :], pb[bi][:, :]),
                        r=(("pb", bi),), w=(("vstg",),))
                    c.dma("gpsimd", vK[h, s, :, :], vstg[:, :], "gp",
                          r=(("vstg",),), w=(("vK", h, s),))
                for h in range(2):
                    c.dma("gpsimd", qT[h, :, s * 512:(s + 1) * 512],
                          qkvstg[:, h * 512:(h + 1) * 512], "gp",
                          r=(("qkvstg", h),), w=(("qT", h, s),))
                    c.dma("gpsimd", kT[h, :, s * 512:(s + 1) * 512],
                          qkvstg[:, (2 + h) * 512:(3 + h) * 512], "gp",
                          r=(("qkvstg", 2 + h),), w=(("kT", h, s),))
                for _ in range(3):
                    if p0_units:
                        p0_unit(*p0_units.pop(0))
            while p0_units:
                p0_unit(*p0_units.pop(0))
            c.barrier()

            ktb = A(0, 8192)
            vaug = A(8192, 8256)
            qb = [A(16448, 512), A(16960, 512)]
            pbf = [A(17472, 512), A(17984, 512)]
            ostg = [A(18496, 128), A(18624, 128)]
            vaugv = vaug.rearrange("p (k e) -> p k e", e=129)
            for h in range(2):
                kkeys = tuple(("kT", h, s2) for s2 in range(NSTRIP))
                vkeys = tuple(("vK", h, s2) for s2 in range(NSTRIP))
                c.dma("sync", ktb[:, :], kT[h, :, :], "sy",
                      r=kkeys, w=(("ktb",),))
                for s2 in range(NSTRIP):
                    c.dma("sync", vaugv[:, s2 * 4:(s2 + 1) * 4, 0:128],
                          vK[h, s2].rearrange("p (kb d) -> p kb d", d=128),
                          "sy", r=(("vK", h, s2),), w=(("vaug",),))
                c.op("vector", lambda E: E.memset(vaugv[:, :, 128:129], 1.0),
                     r=(), w=(("vaug1",),))
                for s in range(NSTRIP):
                    c.dma("sync", qb[s % 2][:, :],
                          qT[h, :, s * 512:(s + 1) * 512], "sy",
                          r=(("qT", h, s),), w=(("qb", s % 2),))
                    nkc = 4 * s + 4

                    def emit_s(kc, s=s):
                        sbk = kc % 2
                        diag = kc - 4 * s
                        c.grp([lambda E, sbk=sbk, kc=kc, s=s: E.matmul(
                            pb[sbk][:, 0:512],
                            ktb[:, kc * 128:(kc + 1) * 128],
                            qb[s % 2][:, :], start=True, stop=True)],
                            r=(("ktb",), ("qb", s % 2)), w=(("pb", sbk),))
                        o0 = 128 * diag if diag >= 0 else 0
                        c.op("scalar", lambda E, sbk=sbk, kc=kc, o0=o0:
                             E.activation(pbf[kc % 2][:, o0:512],
                                          pb[sbk][:, o0:512],
                                          AF.Exp, bias=ebias[:, 0:1]),
                             r=(("pb", sbk), CONST), w=(("pbf", kc % 2),))
                        if diag >= 0:
                            c.op("vector", lambda E, kc=kc, o0=o0:
                                 E.tensor_mul(
                                     pbf[kc % 2][:, o0:o0 + 128],
                                     pbf[kc % 2][:, o0:o0 + 128],
                                     sb_t128[:, :]),
                                 r=(("pbf", kc % 2), CONST),
                                 w=(("pbf", kc % 2),))

                    def emit_pv(kc, s=s):
                        diag = kc - 4 * s
                        t_lo = max(diag, 0)
                        for t in range(t_lo, 4):
                            stop = (kc == 4 * s + t)
                            c.grp([lambda E, t=t, kc=kc, stop=stop: E.matmul(
                                pb[2 + t][:, 0:129],
                                pbf[kc % 2][:, t * 128:(t + 1) * 128],
                                vaugv[:, kc, :],
                                start=(kc == 0), stop=stop)],
                                r=(("pbf", kc % 2), ("vaug",), ("vaug1",)),
                                w=(("pb", 2 + t),) if kc == 0 else
                                (("pbacc", 2 + t),))

                    emit_s(0)
                    for kc in range(1, nkc):
                        emit_s(kc)
                        emit_pv(kc - 1)
                    emit_pv(nkc - 1)
                    for t in range(4):
                        c.op("vector", lambda E, t=t: E.reciprocal(
                            stat[:, 8:9], pb[2 + t][:, 128:129]),
                            r=(("pb", 2 + t), ("pbacc", 2 + t)),
                            w=(("st", 8),))
                        c.op("vector", lambda E, t=t: E.tensor_scalar(
                            ostg[t % 2][:, :], pb[2 + t][:, 0:128],
                            stat[:, 8:9], None, MUL),
                            r=(("pb", 2 + t), ("pbacc", 2 + t), ("st", 8)),
                            w=(("ostg", t % 2),))
                        c.dma("gpsimd",
                              a2a_i[s * 512 + t * 128:s * 512 + (t + 1) * 128,
                                    h * 128:(h + 1) * 128],
                              ostg[t % 2][:, :], "gp",
                              r=(("ostg", t % 2),), w=(("a2a_i", h, s, t),))

            if c.eng == "gpsimd":
                c.wq("gp")
            with allow_ici_all_to_all():
                if c.eng == "gpsimd":
                    c.E.collective_compute(
                        "AllToAll", mybir.AluOpType.bypass,
                        replica_groups=[list(range(NCORE))],
                        ins=[a2a_i.opt()], outs=[a2a_o.opt()],
                    ).then_inc(c.sems["cc"], 1)
            c.cnt["cc"] += 1
            c._fin(("cc", c.cnt["cc"], "gpsimd"), (), (("a2a_o",),))
            if dbg:
                c.dma("gpsimd", dbg_a2a[:, :], a2a_o[:, :], "gp",
                      r=(("a2a_o",),), w=())
            c.barrier()

            attnT = A(0, 16384)
            yT = A(16384, 16384)
            wob = [A(32768, 2048), A(34816, 2048)]
            abuf = [A(36864, 128), A(36992, 128)]
            sq2 = A(37120, 1024)
            xt2 = A(38144, 2048)
            attv = attnT.rearrange("p (cc t) -> p cc t", t=1024)
            for ch in range(NCH):
                cb, col0 = ch // 2, (ch % 2) * 128
                for g in range(2):
                    bi = g % 2
                    for k in range(4):
                        tt = g * 4 + k
                        c.dma("sync", abuf[tt % 2][:, :],
                              a2a_o[cb * 1024 + tt * 128:
                                    cb * 1024 + (tt + 1) * 128,
                                    col0:col0 + 128], "sy",
                              r=(("a2a_o",),), w=(("abuf", tt % 2),))
                        c.grp([lambda E, bi=bi, k=k, tt=tt: E.transpose(
                            pb[bi][:, k * 128:(k + 1) * 128],
                            abuf[tt % 2][:, :], sb_i128[:, :])],
                            r=(("abuf", tt % 2), CONST), w=(("pb", bi),))
                    c.op("scalar", lambda E, bi=bi, ch=ch, g=g: E.copy(
                        attv[:, ch, g * 512:(g + 1) * 512], pb[bi][:, :]),
                        r=(("pb", bi),), w=(("attnT", ch),))

            woutv = wout.rearrange("(cc p) j -> p cc j", p=128)
            akeys = tuple(("attnT", ch) for ch in range(NCH))
            for jt in range(NCH):
                c.dma("sync",
                      wob[jt % 2].rearrange("p (cc j) -> p cc j", j=128),
                      woutv[:, :, jt * 128:(jt + 1) * 128], "sy",
                      w=(("wob", jt % 2),))
                wv = wob[jt % 2].rearrange("p (cc j) -> p cc j", j=128)
                for half in range(2):
                    bi = 2 * (jt % 2) + half
                    c.grp([(lambda E, bi=bi, wv=wv, ccx=ccx, half=half:
                            E.matmul(
                                pb[bi][:, 0:512], wv[:, ccx, :],
                                attv[:, ccx, half * 512:(half + 1) * 512],
                                start=(ccx == 0), stop=(ccx == NCH - 1)))
                           for ccx in range(NCH)],
                        r=akeys + (("wob", jt % 2),), w=(("pb", bi),))
                    c.op("scalar", lambda E, bi=bi, jt=jt, half=half: E.copy(
                        yT[:, jt * 1024 + half * 512:
                           jt * 1024 + (half + 1) * 512], pb[bi][:, :]),
                        r=(("pb", bi),), w=(("yT", jt),))

            def fm_ln(zread_fn, nch, width, sqbuf, sqkey, rowoff, banks,
                      apply_fn):
                b0, b1, b2, b3 = banks
                for ccx in range(nch):
                    ap, rk = zread_fn(ccx)
                    c.op("scalar", lambda E, ap=ap: E.activation(
                        sqbuf, ap, AF.Square), r=rk, w=(sqkey,))
                    c.grp([lambda E, ap=ap, ccx=ccx: E.matmul(
                        pb[b0][0:1, 0:width], ones_c[:, :], ap,
                        start=(ccx == 0), stop=(ccx == nch - 1))],
                        r=rk + (CONST,),
                        w=((("pb", b0),) if ccx == 0 else (("pbacc", b0),)))
                    c.grp([lambda E, ccx=ccx: E.matmul(
                        pb[b1][0:1, 0:width], ones_c[:, :], sqbuf,
                        start=(ccx == 0), stop=(ccx == nch - 1))],
                        r=(sqkey, CONST),
                        w=((("pb", b1),) if ccx == 0 else (("pbacc", b1),)))
                ro = rows[0:1, rowoff:rowoff + width]
                ro2 = rows[0:1, rowoff + width:rowoff + 2 * width]
                RO, RO2 = ("ro", rowoff), ("ro2", rowoff)
                PB0 = (("pb", b0), ("pbacc", b0))
                PB1 = (("pb", b1), ("pbacc", b1))
                c.op("vector", lambda E: E.tensor_scalar(
                    ro, pb[b0][0:1, 0:width], cst[0:1, 0:1], None, MUL),
                    r=PB0 + (CONST,), w=(RO,))
                c.op("vector", lambda E: E.tensor_scalar(
                    ro2, pb[b1][0:1, 0:width], cst[0:1, 0:1], None, MUL),
                    r=PB1 + (CONST,), w=(RO2,))
                c.op("vector", lambda E: E.tensor_mul(
                    pb[b0][0:1, 0:width], ro, ro),
                    r=(RO,) + PB0, w=(("pb", b0),))
                c.op("vector", lambda E: E.tensor_sub(
                    ro2, ro2, pb[b0][0:1, 0:width]),
                    r=(RO2, ("pb", b0)), w=(RO2,))
                c.op("vector", lambda E: E.tensor_scalar(
                    ro2, ro2, cst[0:1, 1:2], None, ADD),
                    r=(RO2, CONST), w=(RO2,))
                c.op("scalar", lambda E: E.activation(ro2, ro2, AF.Sqrt),
                     r=(RO2,), w=(RO2,))
                c.op("vector", lambda E: E.reciprocal(ro2, ro2),
                     r=(RO2,), w=(RO2,))
                c.op("vector", lambda E: E.tensor_mul(ro, ro, ro2),
                     r=(RO, RO2), w=(RO,))
                c.grp([lambda E: E.matmul(
                    pb[b2][:, 0:width], ones_r[0:1, :], ro2,
                    start=True, stop=True)], r=(RO2, CONST), w=(("pb", b2),))
                c.grp([lambda E: E.matmul(
                    pb[b3][:, 0:width], ones_r[0:1, :], ro,
                    start=True, stop=True)], r=(RO, CONST), w=(("pb", b3),))
                for ccx in range(nch):
                    apply_fn(ccx, pb[b2][:, 0:width], pb[b3][:, 0:width],
                             (("pb", b2), ("pb", b3)))

            for half in range(2):
                hoff = half * 512

                def yslice(ccx, hoff=hoff):
                    return yT[:, ccx * 1024 + hoff:ccx * 1024 + hoff + 512]

                def ap5(ccx, s_ap, ms_ap, bk):
                    c.op("vector", lambda E, ccx=ccx, s_ap=s_ap: E.tensor_mul(
                        yslice(ccx), yslice(ccx), s_ap),
                        r=(("yT", ccx),) + bk, w=(("yT", ccx),))
                    c.op("vector", lambda E, ccx=ccx, ms_ap=ms_ap:
                         E.tensor_sub(yslice(ccx), yslice(ccx), ms_ap),
                         r=(("yT", ccx),) + bk, w=(("yT", ccx),))
                fm_ln(lambda ccx: (yslice(ccx), (("yT", ccx),)),
                      NCH, 512, sq2[:, 0:512], ("sq2",),
                      1024 * half, (4, 5, 6, 7), ap5)
            yv1 = yT.rearrange("p (cc t) -> p cc t", t=1024)
            for tt in range(8):
                c.dma("sync", xt2[:, :], xsl[tt * 128:(tt + 1) * 128, :],
                      "sy", w=(("xt2",),))
                for g in range(4):
                    bi = g % 2
                    c.grp([(lambda E, bi=bi, k=k, g=g: E.transpose(
                        pb[bi][:, k * 128:(k + 1) * 128],
                        xt2[:, (4 * g + k) * 128:(4 * g + k + 1) * 128],
                        sb_i128[:, :])) for k in range(4)],
                        r=(("xt2",), CONST), w=(("pb", bi),))
                    ykeys = tuple(("yT", 4 * g + k) for k in range(4))
                    c.op("vector", lambda E, bi=bi, g=g, tt=tt:
                         E.tensor_tensor(
                             yv1[:, 4 * g:4 * g + 4, tt * 128:(tt + 1) * 128],
                             yv1[:, 4 * g:4 * g + 4, tt * 128:(tt + 1) * 128],
                             pb[bi][:, :], ADD),
                         r=ykeys + (("pb", bi),), w=ykeys)
            c.dma("gpsimd", rT.rearrange("(cc p) t -> p cc t", p=128),
                  yT.rearrange("p (cc t) -> p cc t", t=1024), "gp",
                  r=tuple(("yT", j) for j in range(NCH)), w=(("rT",),))
            c.barrier()

            rsl = A(0, 4096)
            zb = A(4096, 4096)
            h2 = A(8192, 16384)
            wb = [A(24576, 2048), A(26624, 2048),
                  A(28672, 2048), A(30720, 2048)]
            hh = [A(32768, 256), A(42496, 256)]
            sq6 = [A(33024, 256), A(42752, 256)]
            tmp6 = A(33280, 256)
            ostg2 = A(33536, 2048)
            z1 = A(35584, 4096)
            rslv = rsl.rearrange("p (cc t) -> p cc t", t=FT)
            zv = zb.rearrange("p (cc t) -> p cc t", t=FT)
            z1v = z1.rearrange("p (cc t) -> p cc t", t=FT)
            y2v = zb.rearrange("p (cc t) -> p cc t", t=FT)
            h2v = h2.rearrange("p (ic t) -> p ic t", t=FT)
            wupHv = wupH.rearrange("(cc p) i -> p cc i", p=128)
            wupLv = wupL.rearrange("(cc p) i -> p cc i", p=128)
            wgHv = wgH.rearrange("(cc p) i -> p cc i", p=128)
            wgLv = wgL.rearrange("(cc p) i -> p cc i", p=128)
            z1hv = sgh_t[:, :].rearrange("p (cc t) -> p cc t", t=FT)
            z1lv = sgl_t[:, :].rearrange("p (cc t) -> p cc t", t=FT)
            wdTv = wdT.rearrange("(ic p) j -> p ic j", p=128)
            rTv = rT.rearrange("(cc p) t -> p cc t", p=128)
            zbkeys = tuple(("zb", ccx) for ccx in range(NCH))

            for sub in range(NSUB):
                toff = sub * FT
                c.dma("sync", rslv, rTv[:, :, toff:toff + FT], "sy",
                      r=(("rT",),), w=(("rsl",),))

                def apz(ccx, s_ap, ms_ap, bk):
                    c.op("vector", lambda E, ccx=ccx, s_ap=s_ap:
                         E.tensor_tensor(zv[:, ccx, :], rslv[:, ccx, :],
                                         s_ap, MUL),
                         r=(("rsl",),) + bk, w=(("zb", ccx),))
                    c.op("vector", lambda E, ccx=ccx, ms_ap=ms_ap:
                         E.tensor_sub(zv[:, ccx, :], zv[:, ccx, :], ms_ap),
                         r=(("zb", ccx),) + bk, w=(("zb", ccx),))
                    c.op("vector", lambda E, ccx=ccx: E.tensor_scalar(
                        zv[:, ccx, :], zv[:, ccx, :], sb_su[:, ccx:ccx + 1],
                        None, MUL), r=(("zb", ccx), CONST), w=(("zb", ccx),))
                fm_ln(lambda ccx: (rslv[:, ccx, :], (("rsl",),)),
                      NCH, FT, sq6[0][:, :], ("sq6", 0), 2048,
                      (0, 1, 2, 3), apz)
                for ccx in range(NCH):
                    bi = 4 + ccx % 2
                    c.grp([lambda E, bi=bi, ccx=ccx: E.matmul(
                        pb[bi][:, 0:FT], sb_h128a[:, :], zv[:, ccx, :],
                        start=True, stop=True)],
                        r=(("zb", ccx), CONST), w=(("pb", bi),))
                    c.op("scalar", lambda E, bi=bi, ccx=ccx: E.copy(
                        z1v[:, ccx, :], pb[bi][:, 0:FT]),
                        r=(("pb", bi),), w=(("z1", ccx),))
                for st in range(4):
                    bit = 1 << st
                    for i in range(NCH):
                        if i & bit:
                            continue
                        j = i | bit
                        c.op("vector", lambda E, i=i, j=j: E.tensor_sub(
                            tmp6[:, :], z1v[:, i, :], z1v[:, j, :]),
                            r=(("z1", i), ("z1", j)), w=(("tmp6",),))
                        c.op("vector", lambda E, i=i, j=j: E.tensor_add(
                            z1v[:, i, :], z1v[:, i, :], z1v[:, j, :]),
                            r=(("z1", i), ("z1", j)), w=(("z1", i),))
                        c.op("vector", lambda E, j=j: E.tensor_copy(
                            z1v[:, j, :], tmp6[:, :]),
                            r=(("tmp6",),), w=(("z1", j),))
                z1keys = tuple(("z1", ccx) for ccx in range(NCH))
                c.op("scalar", lambda E: E.copy(sgh_t[:, :], z1[:, :]),
                     r=z1keys, w=(("z1h",),))
                c.op("vector", lambda E: E.tensor_sub(
                    sgl_t[:, :], z1[:, :], sgh_t[:, :]),
                    r=z1keys + (("z1h",),), w=(("z1l",),))
                zsplit = (("z1h",), ("z1l",))
                stage_of = [-1] * NIC

                def bfly_try(start_c):
                    work = [start_c]
                    while work:
                        i0 = work.pop()
                        st = stage_of[i0]
                        if st < 0 or st >= 6:
                            continue
                        partner = i0 ^ (1 << st)
                        if stage_of[partner] != st:
                            continue
                        i, j = min(i0, partner), max(i0, partner)
                        c.op("vector", lambda E, i=i, j=j: E.tensor_sub(
                            tmp6[:, :], h2v[:, i, :], h2v[:, j, :]),
                            r=(("h2", i), ("h2", j)), w=(("tmp6",),))
                        c.op("vector", lambda E, i=i, j=j: E.tensor_add(
                            h2v[:, i, :], h2v[:, i, :], h2v[:, j, :]),
                            r=(("h2", i), ("h2", j)), w=(("h2", i),))
                        c.op("vector", lambda E, j=j: E.tensor_copy(
                            h2v[:, j, :], tmp6[:, :]),
                            r=(("tmp6",),), w=(("h2", j),))
                        stage_of[i] = st + 1
                        stage_of[j] = st + 1
                        work.append(i)
                        work.append(j)

                def emit_h2(ic):
                    bh = 4 + ic % 2
                    c.grp([lambda E, bh=bh, ic=ic: E.matmul(
                        pb[bh][:, 0:FT], sb_h128b[:, :], hh[ic % 2][:, :],
                        start=True, stop=True)],
                        r=(("hh", ic % 2), CONST), w=(("pb", bh),))
                    c.op("scalar", lambda E, bh=bh, ic=ic: E.copy(
                        h2v[:, ic, :], pb[bh][:, 0:FT]),
                        r=(("pb", bh),), w=(("h2", ic),))
                    stage_of[ic] = 0
                    bfly_try(ic)

                for ic in range(NIC):
                    ubB = wb[ic % 2].bitcast(BF16)
                    gbB = wb[2 + ic % 2].bitcast(BF16)
                    c.dma("sync",
                          ubB[:, 0:2048].rearrange("p (cc j) -> p cc j",
                                                   j=128),
                          wupHv[:, :, ic * 128:(ic + 1) * 128], "sy",
                          r=(("wT", 0, ic // 4, 0, "stgh", 0), ("wT", 0, ic // 4, 0, "stgh", 1)),
                          w=(("wb", ic % 2),))
                    c.dma("sync",
                          ubB[:, 2048:4096].rearrange("p (cc j) -> p cc j",
                                                      j=128),
                          wupLv[:, :, ic * 128:(ic + 1) * 128], "sy",
                          r=(("wT", 0, ic // 4, 0, "stgl", 0), ("wT", 0, ic // 4, 0, "stgl", 1)),
                          w=(("wbl", ic % 2),))
                    c.dma("sync",
                          gbB[:, 0:2048].rearrange("p (cc j) -> p cc j",
                                                   j=128),
                          wgHv[:, :, ic * 128:(ic + 1) * 128], "sy",
                          r=(("wT", 1, ic // 4, 0, "stgh", 0), ("wT", 1, ic // 4, 0, "stgh", 1)),
                          w=(("wb", 2 + ic % 2),))
                    c.dma("sync",
                          gbB[:, 2048:4096].rearrange("p (cc j) -> p cc j",
                                                      j=128),
                          wgLv[:, :, ic * 128:(ic + 1) * 128], "sy",
                          r=(("wT", 1, ic // 4, 0, "stgl", 0), ("wT", 1, ic // 4, 0, "stgl", 1)),
                          w=(("wbl", 2 + ic % 2),))
                    buH = ubB[:, 0:2048].rearrange("p (cc j) -> p cc j",
                                                   j=128)
                    buL = ubB[:, 2048:4096].rearrange("p (cc j) -> p cc j",
                                                      j=128)
                    bgH = gbB[:, 0:2048].rearrange("p (cc j) -> p cc j",
                                                   j=128)
                    bgL = gbB[:, 2048:4096].rearrange("p (cc j) -> p cc j",
                                                      j=128)
                    bu, bg = ic % 2, 2 + ic % 2

                    def x3(bank, wH, wL):
                        fns = []
                        for ccx in range(NCH):
                            for (wv, zv3) in ((wH, z1hv), (wH, z1lv),
                                              (wL, z1hv)):
                                first = (ccx == 0 and wv is wH
                                         and zv3 is z1hv)
                                last = (ccx == NCH - 1 and wv is wL)
                                fns.append(
                                    lambda E, bank=bank, wv=wv, zv3=zv3,
                                    ccx=ccx, first=first, last=last:
                                    E.matmul(pb[bank][:, 0:FT],
                                             wv[:, ccx, :], zv3[:, ccx, :],
                                             start=first, stop=last))
                        return fns
                    c.grp(x3(bu, buH, buL),
                          r=zsplit + (("wb", bu), ("wbl", bu)),
                          w=(("pb", bu),))
                    c.grp(x3(bg, bgH, bgL),
                          r=zsplit + (("wb", bg), ("wbl", bg)),
                          w=(("pb", bg),))
                    c.op("scalar", lambda E, bg=bg, ic=ic: E.activation(
                        sq6[ic % 2][:, :], pb[bg][:, 0:FT], AF.Silu,
                        scale=sb_wg[:, ic:ic + 1]),
                        r=(("pb", bg), CONST), w=(("sq6", ic % 2),))
                    c.op("vector", lambda E, bu=bu, ic=ic: E.tensor_scalar(
                        hh[ic % 2][:, :], pb[bu][:, 0:FT],
                        sb_wu[:, ic:ic + 1], sb_sud[:, ic:ic + 1], MUL, MUL),
                        r=(("pb", bu), CONST), w=(("hh", ic % 2),))
                    c.op("vector", lambda E, ic=ic: E.tensor_mul(
                        hh[ic % 2][:, :], hh[ic % 2][:, :],
                        sq6[ic % 2][:, :]),
                        r=(("hh", ic % 2), ("sq6", ic % 2)),
                        w=(("hh", ic % 2),))
                    if ic > 0:
                        emit_h2(ic - 1)
                emit_h2(NIC - 1)
                assert all(v == 6 for v in stage_of)
                h2keys = tuple(("h2", ic) for ic in range(NIC))
                for jt in range(NCH):
                    bd = 6 + jt % 2
                    for q in range(4):
                        c.dma("sync",
                              wb[q % 2].rearrange("p (ic j) -> p ic j",
                                                  j=128),
                              wdTv[:, q * 16:(q + 1) * 16,
                                   jt * 128:(jt + 1) * 128], "sy",
                              r=(("wT", 2, jt // 4, q),),
                              w=(("wb", q % 2),))
                        wdv = wb[q % 2].rearrange("p (ic j) -> p ic j", j=128)
                        c.grp([(lambda E, bd=bd, wdv=wdv, icq=icq, q=q:
                                E.matmul(
                                    pb[bd][:, 0:FT], wdv[:, icq, :],
                                    h2v[:, q * 16 + icq, :],
                                    start=(q * 16 + icq == 0),
                                    stop=(q * 16 + icq == NIC - 1)))
                               for icq in range(16)],
                            r=h2keys + (("wb", q % 2),),
                            w=((("pb", bd),) if q == 0 else
                               (("pbacc", bd),)))
                    c.op("vector", lambda E, jt=jt, bd=bd: E.tensor_scalar(
                        y2v[:, jt, :], pb[bd][:, 0:FT],
                        sb_wd[:, jt:jt + 1], None, MUL),
                        r=(("pb", bd), ("pbacc", bd), CONST),
                        w=(("zb", jt),))

                def apo(ccx, s_ap, ms_ap, bk):
                    c.op("vector", lambda E, ccx=ccx, s_ap=s_ap: E.tensor_mul(
                        y2v[:, ccx, :], y2v[:, ccx, :], s_ap),
                        r=(("zb", ccx),) + bk, w=(("zb", ccx),))
                    c.op("vector", lambda E, ccx=ccx, ms_ap=ms_ap:
                         E.tensor_sub(y2v[:, ccx, :], y2v[:, ccx, :], ms_ap),
                         r=(("zb", ccx),) + bk, w=(("zb", ccx),))
                    c.op("vector", lambda E, ccx=ccx: E.tensor_tensor(
                        y2v[:, ccx, :], y2v[:, ccx, :], rslv[:, ccx, :],
                        ADD), r=(("zb", ccx), ("rsl",)), w=(("zb", ccx),))
                fm_ln(lambda ccx: (y2v[:, ccx, :], (("zb", ccx),)),
                      NCH, FT, sq6[0][:, :], ("sq6", 0), 2048,
                      (0, 1, 2, 3), apo)
                for tt2 in range(2):
                    for g in range(4):
                        bi = 4 + g % 2
                        zkeys = tuple(("zb", 4 * g + k) for k in range(4))
                        c.grp([(lambda E, bi=bi, k=k, g=g, tt2=tt2:
                                E.transpose(
                                    pb[bi][:, k * 128:(k + 1) * 128],
                                    y2v[:, 4 * g + k,
                                        tt2 * 128:(tt2 + 1) * 128],
                                    sb_i128[:, :])) for k in range(4)],
                            r=zkeys + (CONST,), w=(("pb", bi),))
                        c.op("scalar", lambda E, bi=bi, g=g: E.copy(
                            ostg2[:, g * 512:(g + 1) * 512], pb[bi][:, :]),
                            r=(("pb", bi),), w=(("ostg2", g),))
                    c.dma("gpsimd",
                          out[toff + tt2 * 128:toff + (tt2 + 1) * 128, :],
                          ostg2[:, :], "gp",
                          r=tuple(("ostg2", g) for g in range(4)), w=())
            if dbg == 2:
                c.barrier()
                for name, (src, dst) in dbg_out.items():
                    nd = len(src.tensor.shape)
                    sl = (slice(None),) * nd
                    c.dma("gpsimd", dst[sl], src[sl], "gp")
            if c.eng == "gpsimd":
                c.wq("gp")

        with nc.Block() as block:
            @block.sync
            def _(e):
                emit(Ctx("sync", e, sems))

            @block.tensor
            def _(e):
                emit(Ctx("tensor", e, sems))

            @block.vector
            def _(e):
                emit(Ctx("vector", e, sems))

            @block.scalar
            def _(e):
                emit(Ctx("scalar", e, sems))

            @block.gpsimd
            def _(e):
                emit(Ctx("gpsimd", e, sems))

    return nc


def kernel(L0_attn_x, L0_attn_w_qkv, L0_attn_w_out, L0_ffn_SU_ug, L0_ffn_SU_dp,
           L0_ffn_Wscale_ug, L0_ffn_Wscale_dp, L0_ffn_W_up, L0_ffn_W_gate,
           L0_ffn_W_down):
    cont = np.ascontiguousarray
    x = cont(np.asarray(L0_attn_x, dtype=np.float32).reshape(SEQ, DM))
    wqkv = np.asarray(L0_attn_w_qkv, dtype=np.float32)
    base = {
        "x": x,
        "wout": cont(np.asarray(L0_attn_w_out, dtype=np.float32)),
        "wup": cont(np.asarray(L0_ffn_W_up, dtype=np.float32)),
        "wgate": cont(np.asarray(L0_ffn_W_gate, dtype=np.float32)),
        "wdown": cont(np.asarray(L0_ffn_W_down, dtype=np.float32)),
        "su_ug": cont(np.asarray(L0_ffn_SU_ug, dtype=np.float32)
                      .reshape(NCH, 128).T),
        "su_dp": cont(np.asarray(L0_ffn_SU_dp, dtype=np.float32)
                      .reshape(NIC, 128).T),
        "wsc_u": cont(np.asarray(L0_ffn_Wscale_ug, dtype=np.float32)[:INTER]
                      .reshape(NIC, 128).T),
        "wsc_g": cont(np.asarray(L0_ffn_Wscale_ug, dtype=np.float32)[INTER:]
                      .reshape(NIC, 128).T),
        "wsc_dp": cont(np.asarray(L0_ffn_Wscale_dp, dtype=np.float32)
                       .reshape(NCH, 128).T),
        "i128": np.eye(128, dtype=np.float32),
        "t128": (np.arange(128)[None, :] >=
                 np.arange(128)[:, None]).astype(np.float32),
        "h128a": hadamard(128) / np.float32(np.sqrt(2048.0)),
        "h128b": hadamard(128) / np.float32(np.sqrt(8192.0)),
    }
    nc = build_nc()
    in_maps = []
    for cidx in range(NCORE):
        h0, h1 = 2 * cidx, 2 * cidx + 1
        wq_s = cont(np.concatenate(
            [wqkv[:, h0 * HD:(h0 + 1) * HD],
             wqkv[:, h1 * HD:(h1 + 1) * HD],
             wqkv[:, DM + h0 * HD:DM + (h0 + 1) * HD],
             wqkv[:, DM + h1 * HD:DM + (h1 + 1) * HD],
             wqkv[:, 2 * DM + h0 * HD:2 * DM + (h0 + 1) * HD],
             wqkv[:, 2 * DM + h1 * HD:2 * DM + (h1 + 1) * HD]], axis=1))
        m = dict(base)
        m["wqkv"] = wq_s
        m["xsl"] = cont(x[cidx * TOK:(cidx + 1) * TOK, :])
        in_maps.append(m)
    res = run_bass_kernel_spmd(nc, in_maps, core_ids=list(range(NCORE)))
    outp = np.concatenate([res.results[cidx]["out"] for cidx in range(NCORE)],
                          axis=0)
    return outp.reshape(1, SEQ, DM)
